# version 08
# baseline (speedup 1.0000x reference)
import numpy as np
import concourse.bass as bass
import concourse.mybir as mybir
from concourse.bass_utils import run_bass_kernel_spmd

F32 = mybir.dt.float32
BF16 = mybir.dt.bfloat16
AF = mybir.ActivationFunctionType
ALU = mybir.AluOpType
AX = mybir.AxisListType

D = 2048
NCH = 16
SEQ = 16384
BATCH = 2
TCORE = 4096
NCHUNK = 32
NHEAD = 16
NEXP = 8
FMOE = 1408
FCM = 11
FSH = 5632
NSHBLK = 4
NBLK = NSHBLK + NEXP
TILE = 1024
NTILE = 4
LN_EPS = 1e-5
Z_EPS = 1e-6
KVG = 8
POOL_N = 8

DO_PASSB = True
DBG_TAPS = False
DEV_HALVES = 2
DEV_CHUNKS = NCHUNK
DEV_CC = True
DEV_CORES = 8
DEV_KV_COPY = False


class Em:
    def __init__(self, nc):
        self.nc = nc
        self.S = {}
        self.C = {}

    def sem(self, name):
        if name not in self.S:
            self.S[name] = self.nc.alloc_semaphore(name)
            self.C[name] = 0
        return self.S[name]

    def inc(self, ins, name, n=1):
        ins.then_inc(self.sem(name), n)
        self.C[name] += n
        return self.C[name]

    def w(self, eng, name, val):
        if val:
            eng.wait_ge(self.sem(name), val)


def build_core_program():
    nc = bass.Bass("TRN2", target_bir_lowering=False)

    def din(name, shape, dt=F32):
        return nc.dram_tensor(name, list(shape), dt, kind="ExternalInput").ap()

    x_fm = din("x_fm", (D, TCORE))
    pos_row = din("pos_row", (1, TCORE))
    pos_cols = din("pos_cols", (128, NCHUNK))
    wqT = din("wqT", (D, D))
    wkT = din("wkT", (D, D))
    wvT = din("wvT", (D, D))
    woT = din("woT", (D, D))
    bq_cols = din("bq_cols", (128, NCH))
    bo_cols = din("bo_cols", (128, NCH))
    bkv_rows = din("bkv_rows", (1, 2 * D))
    gate_cols = din("gate_cols", (128, NCH * NEXP))
    sg_cols = din("sg_cols", (128, NCH))
    fc1T = din("fc1T", (NEXP, D, FMOE))
    fc2T = din("fc2T", (NEXP, D, FMOE))
    projT = din("projT", (NEXP, FMOE, D))
    sfc1T = din("sfc1T", (D, FSH))
    sfc2T = din("sfc2T", (D, FSH))
    sprojT = din("sprojT", (FSH, D))
    ones_col_d = din("ones_col", (128, 1))
    ones_row_d = din("ones_row_d", (1, 128))
    ident_d = din("ident_d", (128, 128))
    oh8_d = din("oh8_d", (8, NEXP * 128))

    y_fm = nc.dram_tensor("y_fm", [D, TCORE], F32, kind="ExternalOutput").ap()
    xn_kind = "ExternalOutput" if DBG_TAPS else "Internal"
    xn_scr = nc.dram_tensor("xn_scr", [D, TCORE], BF16, kind=xn_kind).ap()
    cc_in = nc.dram_tensor("cc_in", [128, NHEAD * 132], F32, kind="Internal").ap()
    cc_out = nc.dram_tensor("cc_out", [128, NHEAD * 132], F32,
                            kind="Internal").ap()
    kv_tap = None
    if DBG_TAPS:
        kv_tap = nc.dram_tensor("kv_tap", [128, NHEAD * 132], F32,
                                kind="ExternalOutput").ap()

    x_r = x_fm.rearrange("(c p) t -> p c t", p=128)
    xn_r = xn_scr.rearrange("(c p) t -> p c t", p=128)
    y_r = y_fm.rearrange("(c p) t -> p c t", p=128)
    wq_r = wqT.rearrange("(c p) d -> p c d", p=128)
    wk_r = wkT.rearrange("(c p) f -> p c f", p=128)
    wv_r = wvT.rearrange("(c p) f -> p c f", p=128)
    wo_r = woT.rearrange("(c p) d -> p c d", p=128)
    fc1_r = fc1T.rearrange("e (c p) f -> e p c f", p=128)
    fc2_r = fc2T.rearrange("e (c p) f -> e p c f", p=128)
    proj_r = projT.rearrange("e (fc p) c -> e p fc c", p=128)
    sfc1_r = sfc1T.rearrange("(c p) f -> p c f", p=128)
    sfc2_r = sfc2T.rearrange("(c p) f -> p c f", p=128)
    sproj_r = sprojT.rearrange("(fc p) c -> p fc c", p=128)

    A = nc.alloc_sbuf_tensor
    pool = [A(f"pool{i}", [128, 2048], BF16) for i in range(POOL_N)]
    kv_bf = A("kv_bf", [128, NHEAD * 132], BF16)
    biasq = A("biasq", [128, NCH], F32)
    biaso = A("biaso", [128, NCH], F32)
    dcols = A("dcols", [128, NCHUNK], F32)
    pcols = A("pcols", [128, NCHUNK], F32)
    gate_b = A("gate_b", [128, NCH * NEXP], BF16)
    sg_b = A("sg_b", [128, NCH], BF16)
    onec = A("onec", [128, 1], BF16)
    identb = A("identb", [128, 128], BF16)
    oh8 = A("oh8", [8, NEXP * 128], BF16)
    rows_b = A("rows_b", [128, 4224], BF16)
    rows_f = A("rows_f", [1, 1664], F32)

    P = nc.alloc_psum_tensor
    pq = [P(f"pq{i}", [128, 1024], F32) for i in range(4)]
    pq_bf = [t.ap().bitcast(BF16) for t in pq]

    xn_t_h = A("xn_t", [128, NCH * TILE], BF16)

    E = Em(nc)
    L = {}
    onesr = rows_b.ap()[0:1, 0:128]

    with nc.Block() as block:
        pool_state = dict(fill=[0] * POOL_N, free=[0] * POOL_N, rr=[0])

        def pool_load(src_ap):
            s = pool_state["rr"][0]
            pool_state["rr"][0] = (s + 1) % POOL_N
            need = pool_state["free"][s]
            a, b = src_ap.shape[1], src_ap.shape[2]
            dst = pool[s].ap()[:, 0:a * b].rearrange("p (a b) -> p a b", a=a)

            def fg(g, s=s, need=need, src_ap=src_ap, dst=dst):
                E.w(g, "pe", need)
                E.inc(g.dma_start(dst, src_ap), f"wl{s}", 16)

            block.gpsimd(fg)
            pool_state["fill"][s] += 1
            return s, 16 * pool_state["fill"][s]

        def pool_done(s):
            pool_state["free"][s] = E.C["pe"]

        def setup_sync(s):
            for dst, src in ((biasq.ap(), bq_cols), (biaso.ap(), bo_cols),
                             (pcols.ap(), pos_cols)):
                E.inc(s.dma_start(dst, src), "set", 16)

        def setup_gp(g):
            for dst, src in (
                (gate_b.ap(), gate_cols),
                (sg_b.ap(), sg_cols),
                (onec.ap(), ones_col_d),
                (rows_b.ap()[0:1, 0:128], ones_row_d),
                (identb.ap(), ident_d),
                (oh8.ap(), oh8_d),
            ):
                E.inc(g.dma_start(dst, src), "setg", 16)

        block.sync(setup_sync)
        block.gpsimd(setup_gp)
        set_done = E.C["set"]
        setg_done = E.C["setg"]

        def setup_act(sc):
            E.w(sc, "set", set_done)
            E.w(sc, "setg", setg_done)
            ins = sc.activation(dcols.ap(), pcols.ap(), AF.Exp,
                                scale=-1.0 / SEQ)
            L["dcols"] = E.inc(ins, "act")

        block.scalar(setup_act)

        with (
            nc.sbuf_tensor("wkh", [128, NCH * 1024], BF16) as wkh,
            nc.sbuf_tensor("wvh", [128, NCH * 1024], BF16) as wvh,
            nc.sbuf_tensor("xb0", [128, D], BF16) as xb0,
            nc.sbuf_tensor("xb1", [128, D], BF16) as xb1,
            nc.sbuf_tensor("xq0", [128, D], BF16) as xq0,
            nc.sbuf_tensor("xq1", [128, D], BF16) as xq1,
            nc.sbuf_tensor("xn0", [128, D], BF16) as xn0,
            nc.sbuf_tensor("xn1", [128, D], BF16) as xn1,
            nc.sbuf_tensor("tmpa", [128, D], BF16) as tmpa,
            nc.sbuf_tensor("ksb0", [128, 1024], BF16) as ksb0,
            nc.sbuf_tensor("ksb1", [128, 1024], BF16) as ksb1,
            nc.sbuf_tensor("vsb0", [128, 1024], BF16) as vsb0,
            nc.sbuf_tensor("vsb1", [128, 1024], BF16) as vsb1,
            nc.sbuf_tensor("kv_sb", [128, NHEAD * 132], F32) as kv_sb,
            nc.sbuf_tensor("bkv_b", [1, 2 * D], BF16) as bkv_b,
            nc.sbuf_tensor("mrbc", [128, 256], BF16) as mrbc,
        ):
            xb = [xb0, xb1]
            xq = [xq0, xq1]
            xn2 = [xn0, xn1]
            ksb = [ksb0, ksb1]
            vsb = [vsb0, vsb1]
            kv_sb3 = kv_sb.ap().rearrange("p (h w) -> p h w", h=NHEAD)

            st_sx = pq[0].ap()[0:1, 0:128]
            st_sq = pq[0].ap()[0:1, 128:256]
            m_bc = pq[0].ap()[:, 512:640]
            r_bc = pq[0].ap()[:, 640:768]
            ks_p = pq[0].ap()[:, 768:784]
            k_ps = pq[1].ap()
            v_ps = pq[2].ap()
            kv_ps = pq[3].ap()

            RF = rows_f.ap()
            r_m = RF[0:1, 0:128]
            r_msq = RF[0:1, 128:256]
            r_var = RF[0:1, 256:384]
            r_lnv = RF[0:1, 384:512]
            r_rstd = RF[0:1, 512:640]

            def xload(i):
                p = i % 2
                srcx = x_r[:, :, i * 128:(i + 1) * 128]

                def fg(g, i=i, p=p, srcx=srcx):
                    E.w(g, "pe", L.get(("stats", i - 2)))
                    E.w(g, "dve", L.get(("xsq", i - 2)))
                    E.w(g, "dve", L.get(("xnsub", i - 2)))
                    E.inc(g.dma_start(
                        xb[p].ap().rearrange("p (c t) -> p c t", c=NCH),
                        srcx), f"sxb{p}", 16)

                block.gpsimd(fg)

            def xnload(i):
                p = i % 2

                def fs(s, i=i, p=p):
                    E.w(s, "pe", L.get(("kvproj", i - 2)))
                    L[("xnld", i)] = E.inc(s.dma_start(
                        xn2[p].ap().rearrange("p (c t) -> p c t", c=NCH),
                        xn_r[:, :, i * 128:(i + 1) * 128]), f"sxn{p}", 16)

                block.sync(fs)

            xload(0)
            xload(1)
            for i in range(DEV_CHUNKS):
                pi = i % 2

                def f_xsq(v, i=i, pi=pi):
                    E.w(v, f"sxb{pi}", 16 * (i // 2 + 1))
                    E.w(v, "pe", L.get(("stats", i - 2)))
                    ins = v.tensor_mul(xq[pi].ap(), xb[pi].ap(),
                                       xb[pi].ap())
                    L[("xsq", i)] = E.inc(ins, "dve")

                block.vector(f_xsq)

                def f_stats(t, i=i, pi=pi):
                    E.w(t, f"sxb{pi}", 16 * (i // 2 + 1))
                    E.w(t, "set", set_done)
                    E.w(t, "setg", setg_done)
                    E.w(t, "dve", L.get(("var", i - 1)))
                    for c in range(NCH):
                        t.matmul(st_sx, onec.ap(),
                                 xb[pi].ap()[:, c * 128:(c + 1) * 128],
                                 start=(c == 0), stop=(c == NCH - 1))
                    E.w(t, "dve", L[("xsq", i)])
                    ins = None
                    for c in range(NCH):
                        ins = t.matmul(
                            st_sq, onec.ap(),
                            xq[pi].ap()[:, c * 128:(c + 1) * 128],
                            start=(c == 0), stop=(c == NCH - 1))
                    L[("stats", i)] = E.inc(ins, "pe")

                block.tensor(f_stats)

                def f_rows1(v, i=i):
                    E.w(v, "pe", L[("stats", i)])
                    ins = v.tensor_scalar_mul(r_m, st_sx, 1.0 / D)
                    E.inc(ins, "dve")
                    E.w(v, "dve", E.C["dve"])
                    ins = v.tensor_mul(r_msq, r_m, r_m)
                    E.inc(ins, "dve")
                    E.w(v, "dve", E.C["dve"])
                    ins = v.scalar_tensor_tensor(
                        r_var, st_sq, 1.0 / D, r_msq,
                        op0=ALU.mult, op1=ALU.subtract)
                    E.inc(ins, "dve")
                    E.w(v, "dve", E.C["dve"])
                    ins = v.tensor_scalar_add(r_var, r_var, LN_EPS)
                    L[("var", i)] = E.inc(ins, "dve")

                block.vector(f_rows1)

                def f_rstd(sc, i=i):
                    E.w(sc, "dve", L[("var", i)])
                    ins = sc.activation(r_lnv, r_var, AF.Ln)
                    E.inc(ins, "act")
                    E.w(sc, "act", E.C["act"])
                    ins = sc.activation(r_rstd, r_lnv, AF.Exp, scale=-0.5)
                    L[("rstd", i)] = E.inc(ins, "act")

                block.scalar(f_rstd)

                def f_rows2(v, i=i, pi=pi):
                    mb = rows_b.ap()[0:1, 128 + pi * 128:256 + pi * 128]
                    rb = rows_b.ap()[0:1, 384 + pi * 128:512 + pi * 128]
                    E.w(v, "pe", L.get(("bcast", i - 2)))
                    E.w(v, "act", L[("rstd", i)])
                    ins = v.tensor_copy(mb, r_m)
                    E.inc(ins, "dve")
                    ins = v.tensor_copy(rb, r_rstd)
                    L[("rowsbf", i)] = E.inc(ins, "dve")

                block.vector(f_rows2)

                def f_warm(t, i=i):
                    for k in range(3):
                        E.w(t, "dve", L[("xsq", i)])
                        t.matmul(kv_ps[0:64, 0:64], onesr[0:1, 0:64],
                                 onesr[0:1, 0:64], start=True, stop=True)

                block.tensor(f_warm)

                def f_bcast(t, i=i, pi=pi):
                    E.w(t, "act", L.get(("mrc", i - 1)))
                    E.w(t, "dve", L[("rowsbf", i)])
                    mb = rows_b.ap()[0:1, 128 + pi * 128:256 + pi * 128]
                    rb = rows_b.ap()[0:1, 384 + pi * 128:512 + pi * 128]
                    t.matmul(m_bc, onesr, mb, start=True, stop=True)
                    ins = t.matmul(r_bc, onesr, rb, start=True, stop=True)
                    L[("bcast", i)] = E.inc(ins, "pe")

                block.tensor(f_bcast)

                def f_mrc(sc, i=i):
                    E.w(sc, "pe", L[("bcast", i)])
                    E.w(sc, "dve", L.get(("xnmul", i - 1)))
                    ins = sc.copy(mrbc.ap()[:, 0:128], m_bc)
                    E.inc(ins, "act")
                    ins = sc.copy(mrbc.ap()[:, 128:256], r_bc)
                    L[("mrc", i)] = E.inc(ins, "act")

                block.scalar(f_mrc)

                def f_xn(v, i=i, pi=pi):
                    E.w(v, "act", L[("mrc", i)])
                    E.w(v, f"sspill{pi}", 16 * (i // 2))
                    xba = xb[pi].ap().rearrange("p (c t) -> p c t", c=NCH)
                    mb3 = mrbc.ap()[:, 0:128].unsqueeze(1).broadcast_to(
                        (128, NCH, 128))
                    rb3 = mrbc.ap()[:, 128:256].unsqueeze(1).broadcast_to(
                        (128, NCH, 128))
                    tm3 = tmpa.ap().rearrange("p (c t) -> p c t", c=NCH)
                    xn3 = xn2[pi].ap().rearrange("p (c t) -> p c t", c=NCH)
                    with nc.allow_low_precision(
                            reason="xn is a bf16 matmul operand"):
                        ins = v.tensor_sub(tm3, xba, mb3)
                        L[("xnsub", i)] = E.inc(ins, "dve")
                        E.w(v, "dve", E.C["dve"])
                        ins = v.tensor_mul(xn3, tm3, rb3)
                    L[("xnmul", i)] = E.inc(ins, "dve")

                block.vector(f_xn)

                def f_spill(s, i=i, pi=pi):
                    E.w(s, "dve", L[("xnmul", i)])
                    E.inc(s.dma_start(
                        xn_r[:, :, i * 128:(i + 1) * 128],
                        xn2[pi].ap().rearrange("p (c t) -> p c t", c=NCH)),
                        f"sspill{pi}", 16)

                block.sync(f_spill)

                if i + 2 < DEV_CHUNKS:
                    xload(i + 2)

            for half in range(DEV_HALVES):
                HOFF = half * 1024

                bar = {k: E.C.get(k, 0)
                       for k in ("pe", "act", "dve", "sspill0", "sspill1")}

                def f_bar_t(tt, bar=bar):
                    E.w(tt, "act", bar["act"])
                    E.w(tt, "dve", bar["dve"])

                def f_bar_v(v, bar=bar):
                    E.w(v, "pe", bar["pe"])
                    E.w(v, "act", bar["act"])

                def f_bar_a(sc, bar=bar):
                    E.w(sc, "pe", bar["pe"])
                    E.w(sc, "dve", bar["dve"])

                def f_bar_s(s, bar=bar):
                    E.w(s, "pe", bar["pe"])
                    E.w(s, "dve", bar["dve"])
                    E.w(s, "sspill0", bar["sspill0"])
                    E.w(s, "sspill1", bar["sspill1"])

                block.tensor(f_bar_t)
                block.vector(f_bar_v)
                block.scalar(f_bar_a)
                block.sync(f_bar_s)
                Lclear = [k for k in L if isinstance(k, tuple)]
                for k in Lclear:
                    del L[k]

                def wload(g, half=half, HOFF=HOFF):
                    if half == 0:
                        ins = g.memset(kv_sb.ap(), 0.0)
                        E.inc(ins, "kvz", 1)
                        E.inc(g.dma_start(bkv_b.ap(), bkv_rows), "swk", 16)
                    if half == 1:
                        E.w(g, "pe", E.C["pe"])
                    E.inc(g.dma_start(
                        wkh.ap().rearrange("p (c f) -> p c f", c=NCH),
                        wk_r[:, :, HOFF:HOFF + 1024]), "swk", 16)
                    E.inc(g.dma_start(
                        wvh.ap().rearrange("p (c f) -> p c f", c=NCH),
                        wv_r[:, :, HOFF:HOFF + 1024]), "swk", 16)

                block.gpsimd(wload)
                wk_done = E.C["swk"]

                xnload(0)
                xnload(1)

                for i in range(DEV_CHUNKS + 1):
                    j = i
                    jp = i - 1
                    pj = j % 2

                    if j < DEV_CHUNKS:
                        def f_kv(t, j=j, pj=pj, half=half):
                            E.w(t, "swk", wk_done)
                            xnb = xn2[pj]
                            E.w(t, f"sxn{pj}", L[("xnld", j)])
                            E.w(t, "act", L.get(("kevac", j - 1)))
                            for q in range(2):
                                for c in range(NCH):
                                    t.matmul(
                                        k_ps[:, q * 512:(q + 1) * 512],
                                        xnb.ap()[:, c * 128:(c + 1) * 128],
                                        wkh.ap()[:, c * 1024 + q * 512:
                                                 c * 1024 + (q + 1) * 512],
                                        start=(c == 0), stop=False)
                                t.matmul(
                                    k_ps[:, q * 512:(q + 1) * 512],
                                    onesr,
                                    bkv_b.ap()[0:1, half * 1024 + q * 512:
                                               half * 1024 + (q + 1) * 512],
                                    start=False, stop=True)
                            E.w(t, "act", L.get(("vevac", j - 1)))
                            ins = None
                            for q in range(2):
                                for c in range(NCH):
                                    t.matmul(
                                        v_ps[:, q * 512:(q + 1) * 512],
                                        xnb.ap()[:, c * 128:(c + 1) * 128],
                                        wvh.ap()[:, c * 1024 + q * 512:
                                                 c * 1024 + (q + 1) * 512],
                                        start=(c == 0), stop=False)
                                ins = t.matmul(
                                    v_ps[:, q * 512:(q + 1) * 512],
                                    onesr,
                                    bkv_b.ap()[0:1, 2048 + half * 1024 +
                                               q * 512:
                                               2048 + half * 1024 +
                                               (q + 1) * 512],
                                    start=False, stop=True)
                            L[("kvproj", j)] = E.inc(ins, "pe")

                        block.tensor(f_kv)

                        def f_kvev(sc, j=j, pj=pj):
                            E.w(sc, "pe", L[("kvproj", j)])
                            E.w(sc, "act", L["dcols"])
                            E.w(sc, "pe", L.get(("kvmm", j - 2)))
                            ins = sc.activation(ksb[pj].ap(), k_ps, AF.Relu,
                                                scale=dcols.ap()[:, j:j + 1])
                            L[("kevac", j)] = E.inc(ins, "act")
                            ins = sc.copy(vsb[pj].ap(), v_ps)
                            L[("vevac", j)] = E.inc(ins, "act")

                        block.scalar(f_kvev)

                        if j + 2 < DEV_CHUNKS:
                            xnload(j + 2)

                    if 0 <= jp < DEV_CHUNKS:
                        def f_pe2(t, j=jp, pj=jp % 2):
                            E.w(t, "act", L[("kevac", j)])
                            E.w(t, "act", L[("vevac", j)])
                            E.w(t, "dve", L.get(("kvev", j - 1)))
                            ins = None
                            for hh in range(8):
                                ins = t.matmul(
                                    ks_p[:, hh:hh + 1],
                                    ksb[pj].ap()[:, hh * 128:(hh + 1) * 128],
                                    onec.ap(),
                                    start=True, stop=True)
                            for hh in range(8):
                                ins = t.matmul(
                                    kv_ps[:, hh * 128:(hh + 1) * 128],
                                    ksb[pj].ap()[:, hh * 128:(hh + 1) * 128],
                                    vsb[pj].ap()[:, hh * 128:(hh + 1) * 128],
                                    start=True, stop=True)
                            L[("kvmm", j)] = E.inc(ins, "pe")

                        block.tensor(f_pe2)

                        def f_dve2(v, j=jp, half=half):
                            E.w(v, "pe", L[("kvmm", j)])
                            dstks = kv_sb3[:, half * 8:half * 8 + 8, 128:129]
                            srcks = ks_p[:, 0:8].rearrange(
                                "p (h w) -> p h w", w=1)
                            dstkv = kv_sb3[:, half * 8:half * 8 + 8, 0:128]
                            srckv = kv_ps.rearrange("p (h w) -> p h w", h=8)
                            if j == 0:
                                E.w(v, "kvz", 1)
                                ins = v.tensor_copy(dstkv, srckv)
                            else:
                                ins = v.tensor_add(dstkv, dstkv, srckv)
                            E.inc(ins, "dve")
                            E.w(v, "dve", E.C["dve"])
                            if j == 0:
                                ins = v.tensor_copy(dstks, srcks)
                            else:
                                ins = v.tensor_add(dstks, dstks, srcks)
                            L[("kvev", j)] = E.inc(ins, "dve")

                        block.vector(f_dve2)

            if DO_PASSB:
                def f_xn0a(g):
                    E.w(g, "sspill0", E.C.get("sspill0", 0))
                    E.w(g, "sspill1", E.C.get("sspill1", 0))
                    E.inc(g.dma_start(
                        xn_t_h.ap()[:, 0:12 * TILE].rearrange(
                            "p (c u) -> p c u", c=12),
                        xn_r[:, 0:12, 0:TILE]), "sxnB", 16)

                block.gpsimd(f_xn0a)

                def f_xn0b(s):
                    E.w(s, "sspill0", E.C.get("sspill0", 0))
                    E.w(s, "sspill1", E.C.get("sspill1", 0))
                    E.inc(s.dma_start(
                        xn_t_h.ap()[:, 12 * TILE:].rearrange(
                            "p (c u) -> p c u", c=4),
                        xn_r[:, 12:16, 0:TILE]), "sxnC", 16)

                block.sync(f_xn0b)

            def f_cc(g):
                E.w(g, "dve", L[("kvev", DEV_CHUNKS - 1)])
                if not DEV_CC:
                    E.inc(g.dma_start(kv_bf.ap(), kv_sb.ap()), "sccd", 16)
                    if DBG_TAPS:
                        E.inc(g.dma_start(kv_tap, kv_sb.ap()), "sccd", 16)
                    return
                E.inc(g.dma_start(cc_in, kv_sb.ap()), "sccd", 16)
                E.w(g, "sccd", E.C["sccd"])
                ins = g.collective_compute(
                    "AllReduce", ALU.add,
                    replica_groups=[[0, 1, 2, 3], [4, 5, 6, 7]],
                    ins=[cc_in.opt()],
                    outs=[cc_out.opt()],
                )
                E.inc(ins, "sccc", 1)
                E.w(g, "sccc", E.C["sccc"])
                E.inc(g.dma_start(kv_bf.ap(), cc_out), "sccd", 16)
                if DBG_TAPS:
                    E.inc(g.dma_start(kv_tap, cc_out), "sccd", 16)

            wq_pre = {}
            if DO_PASSB:
                for h in range(POOL_N):
                    wq_pre[h] = pool_load(
                        wq_r[:, :, h * 128:(h + 1) * 128])

            block.gpsimd(f_cc)
            cc_done = E.C["sccd"]
            passa_pe = E.C["pe"]

        if DO_PASSB:
            _emit_pass_b(nc, block, E, L, dict(
                x_r=x_r, xn_r=xn_r, y_r=y_r, wq_r=wq_r, wo_r=wo_r,
                gate_b=gate_b, sg_b=sg_b, fc1_r=fc1_r, fc2_r=fc2_r,
                proj_r=proj_r, sfc1_r=sfc1_r, sfc2_r=sfc2_r, sproj_r=sproj_r,
                pos_row=pos_row, pool=pool, kv_bf=kv_bf, biasq=biasq,
                biaso=biaso, rows_b=rows_b, rows_f=rows_f, identb=identb,
                oh8=oh8, pq=pq, pq_bf=pq_bf, cc_done=cc_done, onesr=onesr,
                pool_load=pool_load, pool_done=pool_done, wq_pre=wq_pre,
                xn_t=xn_t_h, passa_pe=passa_pe,
            ))
        else:
            def f_fin(s):
                E.w(s, "sspill0", E.C.get("sspill0", 0))
                E.w(s, "sspill1", E.C.get("sspill1", 0))
                E.w(s, "sccd", E.C.get("sccd", 0))

            block.sync(f_fin)

    return nc


def _emit_pass_b(nc, block, E, L, R):
    pq = R["pq"]
    pq_bf = R["pq_bf"]
    pool = R["pool"]
    kv_bf = R["kv_bf"].ap()
    rows_b = R["rows_b"].ap()
    rows_f = R["rows_f"].ap()
    onesr = R["onesr"]
    cc_done = R["cc_done"]

    A = nc.alloc_sbuf_tensor
    y_acc = A("y_acc", [128, NCH * TILE], F32)
    xn_t = R["xn_t"]
    aoh = A("aoh", [128, NHEAD * TILE], BF16)
    qsb = [A(f"qsb{p}", [128, TILE], BF16) for p in range(2)]
    zbc = [A(f"zbc{p}", [128, TILE], BF16) for p in range(2)]
    silb = [A(f"silb{p}", [128, TILE], BF16) for p in range(2)]
    rarena = A("rarena", [128, 512], F32)
    comb_bf = A("comb_bf", [128, 64], BF16)
    combfm = A("combfm", [8, TILE], BF16)
    xev = [A(f"xev{p}", [128, TILE], F32) for p in range(2)]

    prev_store = {0: 0, 1: 0}
    prev_store_sem = {0: "sst0", 1: "sst1"}
    prev_yev = {0: 0, 1: 0}
    pool_load = R["pool_load"]
    pool_done = R["pool_done"]

    LAST = NBLK - 1

    for t in range(NTILE):
        T0 = t * TILE

        def f_ld0(s, t=t, T0=T0):
            E.w(s, "dve", L.get(("zeps", (NHEAD - 1, t - 1))))
            E.inc(s.dma_start(rows_f[0:1, 640:640 + TILE],
                              R["pos_row"][0:1, T0:T0 + TILE]), "spos", 16)

        block.sync(f_ld0)

        if t > 0:
            wq_slots_pre = {}
            for h in range(2):
                wq_slots_pre[h] = pool_load(
                    R["wq_r"][:, :, h * 128:(h + 1) * 128])

            def f_ld0b(s, t=t, T0=T0):
                E.w(s, "pe", L.get(("xnfree", t - 1)))
                E.inc(s.dma_start(
                    xn_t.ap()[:, 11 * TILE:].rearrange(
                        "p (c u) -> p c u", c=5),
                    R["xn_r"][:, 11:16, T0:T0 + TILE]), "sxnC", 16)

            block.sync(f_ld0b)

            def f_ld0c(g, t=t, T0=T0):
                E.w(g, "pe", L.get(("xnfree", t - 1)))
                E.inc(g.dma_start(
                    xn_t.ap()[:, 0:6 * TILE].rearrange(
                        "p (c u) -> p c u", c=6),
                    R["xn_r"][:, 0:6, T0:T0 + TILE]), "sxnB", 16)

            block.gpsimd(f_ld0c)

            def f_ld0d(sc, t=t, T0=T0):
                E.w(sc, "pe", L.get(("xnfree", t - 1)))
                E.inc(sc.dma_start(
                    xn_t.ap()[:, 6 * TILE:11 * TILE].rearrange(
                        "p (c u) -> p c u", c=5),
                    R["xn_r"][:, 6:11, T0:T0 + TILE]), "sxnD", 16)

            block.scalar(f_ld0d)
        xn_ready = E.C["sxnB"]
        xn_ready2 = E.C["sxnC"]
        xn_ready3 = E.C.get("sxnD", 0)
        pos_ready = E.C["spos"]

        def f_epsa(sc, t=t, pos_ready=pos_ready):
            erow = rows_f[0:1, 640:640 + TILE]
            E.w(sc, "spos", pos_ready)
            E.w(sc, "dve", L.get(("zeps", (NHEAD - 1, t - 1))))
            ins = sc.activation(erow, erow, AF.Exp, scale=1.0 / SEQ)
            L[("epsx", t)] = E.inc(ins, "act")

        block.scalar(f_epsa)

        def f_epsb(v, t=t):
            erow = rows_f[0:1, 640:640 + TILE]
            E.w(v, "act", L[("epsx", t)])
            ins = v.tensor_scalar_mul(erow, erow, Z_EPS)
            L[("epsr", t)] = E.inc(ins, "dve")
            E.w(v, "dve", E.C["dve"])
            E.w(v, "pe", L.get(("zmm", (NHEAD - 1, t - 1))))
            with nc.allow_low_precision(reason="z eps term, tiny relative"):
                ins = v.tensor_copy(
                    rows_b[0:1, 2176 + TILE:2176 + 2 * TILE], erow)
            L[("epsb", t)] = E.inc(ins, "dve")

        block.vector(f_epsb)

        if t == 0:
            wq_slots = dict(R["wq_pre"])
        else:
            wq_slots = wq_slots_pre

        for it in range(NHEAD + 1):
            h = it
            hp = it - 1
            if hp >= 0:
                def f_z(tt, hp=hp, t=t):
                    E.w(tt, "act", L[("qrelu", (hp, t))])
                    E.w(tt, "dve", L.get(("zeps", (hp - 1, t))))
                    if hp == 0:
                        E.w(tt, "act", L.get(("sig", t - 1)))
                        E.w(tt, "dve", L[("epsb", t)])
                    ins = None
                    for n in range(2):
                        tt.matmul(
                            pq[1].ap()[0:1, n * 512:(n + 1) * 512],
                            kv_bf[:, hp * 132 + 128:hp * 132 + 129],
                            qsb[hp % 2].ap()[:, n * 512:(n + 1) * 512],
                            start=True, stop=False)
                        ins = tt.matmul(
                            pq[1].ap()[0:1, n * 512:(n + 1) * 512],
                            onesr[0:1, 0:1],
                            rows_b[0:1, 2176 + TILE + n * 512:
                                   2176 + TILE + (n + 1) * 512],
                            start=False, stop=True)
                    L[("zmm", (hp, t))] = E.inc(ins, "pe")

                block.tensor(f_z)

            if h < NHEAD:
                def f_q(tt, h=h, t=t, xn_ready=xn_ready,
                        xn_ready2=xn_ready2, xn_ready3=xn_ready3):
                    slot, wv = wq_slots[h]
                    E.w(tt, f"wl{slot}", wv)
                    E.w(tt, "sxnB", xn_ready)
                    E.w(tt, "sxnC", xn_ready2)
                    if xn_ready3:
                        E.w(tt, "sxnD", xn_ready3)
                    E.w(tt, "act", L.get(("qrelu", (h - 1, t))))
                    if h == 0:
                        E.w(tt, "dve",
                            L.get(("yadd", (LAST, NCH - 1, t - 1))))
                        E.w(tt, "sccd", cc_done)
                    ins = None
                    for n in range(2):
                        for c in range(NCH):
                            ins = tt.matmul(
                                pq[0].ap()[:, n * 512:(n + 1) * 512],
                                pool[slot].ap()[:, c * 128:(c + 1) * 128],
                                xn_t.ap()[:, c * TILE + n * 512:
                                          c * TILE + (n + 1) * 512],
                                start=(c == 0), stop=(c == NCH - 1))
                    L[("qps", (h, t))] = E.inc(ins, "pe")
                    pool_done(slot)

                block.tensor(f_q)

                nxt = h + (POOL_N if t == 0 else 2)
                if nxt < NHEAD and nxt not in wq_slots:
                    wq_slots[nxt] = pool_load(
                        R["wq_r"][:, :, nxt * 128:(nxt + 1) * 128])

                def f_qrelu(sc, h=h, t=t):
                    E.w(sc, "pe", L[("qps", (h, t))])
                    E.w(sc, "pe", L.get(("smm", (h - 2, t))))
                    ins = sc.activation(qsb[h % 2].ap(), pq[0].ap(), AF.Relu,
                                        bias=R["biasq"].ap()[:, h:h + 1])
                    L[("qrelu", (h, t))] = E.inc(ins, "act")

                block.scalar(f_qrelu)

            if hp >= 0:
                def f_zchain(v, hp=hp, t=t):
                    pz = hp % 2
                    zr_b = rows_b[0:1, 128 + pz * TILE:
                                  128 + (pz + 1) * TILE]
                    E.w(v, "pe", L[("zmm", (hp, t))])
                    E.w(v, "pe", L.get(("zbmm", (hp - 2, t))))
                    with nc.allow_low_precision(reason="z row feeds bf16 mm"):
                        ins = v.reciprocal(zr_b, pq[1].ap()[0:1, 0:TILE])
                    L[("zeps", (hp, t))] = E.inc(ins, "dve")
                    L[("zrec", (hp, t))] = L[("zeps", (hp, t))]

                block.vector(f_zchain)

                def f_zb(tt, hp=hp, t=t):
                    E.w(tt, "dve", L.get(("ao", (hp - 1, t))))
                    if hp == 0:
                        E.w(tt, "dve",
                            L.get(("yadd", (LAST, NCH - 2, t - 1))))
                    ins = None
                    for n in range(2):
                        ins = tt.matmul(
                            pq[3].ap()[:, n * 512:(n + 1) * 512],
                            kv_bf[:, hp * 132:hp * 132 + 128],
                            qsb[hp % 2].ap()[:, n * 512:(n + 1) * 512],
                            start=True, stop=True)
                    L[("smm", (hp, t))] = E.inc(ins, "pe")
                    E.w(tt, "dve", L[("zrec", (hp, t))])
                    E.w(tt, "act", L.get(("zbcc", (hp - 1, t))))
                    if hp == 0:
                        E.w(tt, "dve", L.get(("hsc", (LAST, FCM - 1, t - 1))))
                    pz = hp % 2
                    zr_b = rows_b[0:1, 128 + pz * TILE:
                                  128 + (pz + 1) * TILE]
                    for n in range(2):
                        ins = tt.matmul(
                            pq[2].ap()[:, n * 512:(n + 1) * 512],
                            onesr,
                            zr_b[0:1, n * 512:(n + 1) * 512],
                            start=True, stop=True)
                    L[("zbmm", (hp, t))] = E.inc(ins, "pe")

                block.tensor(f_zb)

                def f_zbc(sc, hp=hp, t=t):
                    E.w(sc, "pe", L[("zbmm", (hp, t))])
                    E.w(sc, "dve", L.get(("ao", (hp - 2, t))))
                    ins = sc.copy(zbc[hp % 2].ap(), pq[2].ap())
                    L[("zbcc", (hp, t))] = E.inc(ins, "act")

                block.scalar(f_zbc)

                def f_ao(v, hp=hp, t=t):
                    E.w(v, "pe", L[("smm", (hp, t))])
                    E.w(v, "act", L[("zbcc", (hp, t))])
                    if hp == 0:
                        E.w(v, "pe", L.get(("womm", (NCH - 1, t - 1))))
                    ins = v.tensor_mul(
                        aoh.ap()[:, hp * TILE:(hp + 1) * TILE],
                        zbc[hp % 2].ap(), pq[3].ap())
                    L[("ao", (hp, t))] = E.inc(ins, "dve")

                block.vector(f_ao)

        wo_slots = {}
        for cc in range(2):
            wo_slots[cc] = pool_load(R["wo_r"][:, :, cc * 128:(cc + 1) * 128])
        for cc in range(NCH):
            if cc + 2 < NCH:
                wo_slots[cc + 2] = pool_load(
                    R["wo_r"][:, :, (cc + 2) * 128:(cc + 3) * 128])

            def f_wo(tt, cc=cc, t=t):
                slot, wv = wo_slots[cc]
                E.w(tt, f"wl{slot}", wv)
                bank = pq[cc % 2].ap()
                E.w(tt, "dve", L.get(("woev", (cc - 2, t))))
                if cc == 0:
                    E.w(tt, "act", L[("qrelu", (NHEAD - 1, t))])
                    E.w(tt, "dve", L[("ao", (NHEAD - 1, t))])
                if cc == 1:
                    E.w(tt, "dve", L[("zeps", (NHEAD - 1, t))])
                ins = None
                for n in range(2):
                    for hc in range(NHEAD):
                        ins = tt.matmul(
                            bank[:, n * 512:(n + 1) * 512],
                            pool[slot].ap()[:, hc * 128:(hc + 1) * 128],
                            aoh.ap()[:, hc * TILE + n * 512:
                                     hc * TILE + (n + 1) * 512],
                            start=(hc == 0), stop=(hc == NHEAD - 1))
                L[("womm", (cc, t))] = E.inc(ins, "pe")
                pool_done(slot)

            block.tensor(f_wo)

            def f_woev(v, cc=cc, t=t):
                E.w(v, "pe", L[("womm", (cc, t))])
                E.w(v, f"sst{cc % 2}", L.get(("yfree", (cc, t - 1)), 0))
                ins = v.tensor_scalar_add(
                    y_acc.ap()[:, cc * TILE:(cc + 1) * TILE],
                    pq[cc % 2].ap(), R["biaso"].ap()[:, cc:cc + 1])
                L[("woev", (cc, t))] = E.inc(ins, "dve")

            block.vector(f_woev)

        def f_router_mm(tt, t=t):
            E.w(tt, "act", L[("zbcc", (NHEAD - 1, t))])
            E.w(tt, "dve", L.get(("comb", t - 1)))
            ins = None
            for t8 in range(8):
                for c in range(NCH):
                    ins = tt.matmul(
                        pq[2].ap()[:, t8 * 8:(t8 + 1) * 8],
                        xn_t.ap()[:, c * TILE + t8 * 128:
                                  c * TILE + (t8 + 1) * 128],
                        R["gate_b"].ap()[:, c * 8:(c + 1) * 8],
                        start=(c == 0), stop=(c == NCH - 1))
            L[("rmm", t)] = E.inc(ins, "pe")
            E.w(tt, "dve", L[("zeps", (NHEAD - 1, t))])
            E.w(tt, "dve", L[("woev", (NCH - 1, t))])
            for n in range(2):
                for c in range(NCH):
                    ins = tt.matmul(
                        pq[1].ap()[0:1, n * 512:(n + 1) * 512],
                        R["sg_b"].ap()[:, c:c + 1],
                        xn_t.ap()[:, c * TILE + n * 512:
                                  c * TILE + (n + 1) * 512],
                        start=(c == 0), stop=(c == NCH - 1))
            L[("gmm", t)] = E.inc(ins, "pe")

        block.tensor(f_router_mm)

        def f_sig(sc, t=t):
            E.w(sc, "pe", L[("gmm", t)])
            E.w(sc, "pe", L.get(("gbmm", t - 1)))
            ins = sc.activation(rows_b[0:1, 2176:2176 + TILE],
                                pq[1].ap()[0:1, 0:TILE], AF.Sigmoid)
            L[("sig", t)] = E.inc(ins, "act")

        block.scalar(f_sig)

        RA = rarena.ap()
        probs3 = RA[:, 0:64].rearrange("p (a b) -> p a b", a=8)
        st1 = RA[:, 64:72].rearrange("p (a b) -> p a b", b=1)
        st2 = RA[:, 72:80].rearrange("p (a b) -> p a b", b=1)
        wrk3 = RA[:, 128:192].rearrange("p (a b) -> p a b", a=8)
        comb3 = RA[:, 192:256].rearrange("p (a b) -> p a b", a=8)

        def f_rsoft1(v, t=t):
            rps = pq[2].ap()[:, 0:64].rearrange("p (a b) -> p a b", a=8)

            E.w(v, "pe", L[("rmm", t)])
            E.w(v, "dve", L.get(("comb_bf", t - 1)))
            ins = v.tensor_reduce(st1, rps, axis=AX.X, op=ALU.max,
                                  negate=True)
            E.inc(ins, "dve")
            E.w(v, "dve", E.C["dve"])
            ins = v.tensor_tensor(probs3, rps,
                                  st1.broadcast_to((128, 8, 8)), op=ALU.add)
            L[("rsub", t)] = E.inc(ins, "dve")

        block.vector(f_rsoft1)

        def f_rexp(sc, t=t):
            E.w(sc, "dve", L[("rsub", t)])
            ins = sc.activation(RA[:, 0:64], RA[:, 0:64], AF.Exp)
            L[("rexp", t)] = E.inc(ins, "act")

        block.scalar(f_rexp)

        def f_rsoft2(v, t=t):
            def step(ins):
                E.inc(ins, "dve")
                E.w(v, "dve", E.C["dve"])

            E.w(v, "act", L[("rexp", t)])
            step(v.tensor_reduce(st1, probs3, axis=AX.X, op=ALU.add))
            step(v.reciprocal(st1, st1))
            step(v.tensor_tensor(probs3, probs3,
                                 st1.broadcast_to((128, 8, 8)), op=ALU.mult))
            step(v.tensor_reduce(st2, probs3, axis=AX.X, op=ALU.max))
            step(v.tensor_tensor(wrk3, probs3,
                                 st2.broadcast_to((128, 8, 8)),
                                 op=ALU.is_equal))
            step(v.scalar_tensor_tensor(wrk3, wrk3, -1e9, probs3,
                                        op0=ALU.mult, op1=ALU.add))
            step(v.tensor_reduce(st2, wrk3, axis=AX.X, op=ALU.max))
            step(v.tensor_tensor(wrk3, probs3,
                                 st2.broadcast_to((128, 8, 8)),
                                 op=ALU.is_ge))
            ins = v.tensor_tensor(comb3, probs3, wrk3, op=ALU.mult)
            L[("comb", t)] = E.inc(ins, "dve")
            E.w(v, "dve", E.C["dve"])
            E.w(v, "pe", L.get(("ctr", t - 1)))
            ins = v.tensor_copy(comb_bf.ap(), RA[:, 192:256])
            L[("comb_bf", t)] = E.inc(ins, "dve")

        block.vector(f_rsoft2)

        def f_ctr(tt, t=t):
            E.w(tt, "dve", L[("woev", (NCH - 2, t))])
            E.w(tt, "dve", L[("comb_bf", t)])
            ins = None
            for t8 in range(8):
                ins = tt.transpose(
                    pq_bf[0][0:8, t8 * 128:(t8 + 1) * 128],
                    comb_bf.ap()[:, t8 * 8:(t8 + 1) * 8],
                    R["identb"].ap())
            L[("ctr", t)] = E.inc(ins, "pe")

        block.tensor(f_ctr)

        def f_cfm(v, t=t):
            E.w(v, "pe", L[("ctr", t)])
            E.w(v, "pe", L.get(("cbmm", (NEXP - 1, t - 1))))
            ins = v.tensor_copy(combfm.ap(), pq_bf[0][0:8, 0:TILE])
            L[("cfm", t)] = E.inc(ins, "dve")

        block.vector(f_cfm)

        def f_gb(tt, t=t):
            E.w(tt, "act", L[("sig", t)])
            E.w(tt, "dve", L[("rsub", t)])
            E.w(tt, "dve", L.get(("hsc", (LAST, FCM - 1, t - 1))))
            ins = None
            for n in range(2):
                ins = tt.matmul(
                    pq[2].ap()[:, n * 512:(n + 1) * 512],
                    onesr, rows_b[0:1, 2176 + n * 512:2176 + (n + 1) * 512],
                    start=True, stop=True)
            L[("gbmm", t)] = E.inc(ins, "pe")

        block.tensor(f_gb)

        def emit_block(bi, t):
            shared = bi < NSHBLK
            e = bi - NSHBLK

            if shared:
                def w1(fc):
                    return R["sfc1_r"][:, :, bi * FMOE + fc * 128:
                                       bi * FMOE + (fc + 1) * 128]

                def w2(fc):
                    return R["sfc2_r"][:, :, bi * FMOE + fc * 128:
                                       bi * FMOE + (fc + 1) * 128]

                def wp(cc):
                    return R["sproj_r"][:, bi * FCM:(bi + 1) * FCM,
                                        cc * 128:(cc + 1) * 128]
            else:
                def w1(fc):
                    return R["fc1_r"][e, :, :, fc * 128:(fc + 1) * 128]

                def w2(fc):
                    return R["fc2_r"][e, :, :, fc * 128:(fc + 1) * 128]

                def wp(cc):
                    return R["proj_r"][e, :, :, cc * 128:(cc + 1) * 128]

            if not shared:
                def f_cb(tt, e=e, bi=bi, t=t):
                    E.w(tt, "dve", L[("cfm", t)])
                    E.w(tt, "dve", L[("hsc", (bi - 1, FCM - 1, t))])
                    ins = None
                    for n in range(2):
                        ins = tt.matmul(
                            pq[2].ap()[:, n * 512:(n + 1) * 512],
                            R["oh8"].ap()[0:8, e * 128:(e + 1) * 128],
                            combfm.ap()[0:8, n * 512:(n + 1) * 512],
                            start=True, stop=True)
                    L[("cbmm", (e, t))] = E.inc(ins, "pe")

                block.tensor(f_cb)
                scale_lm = ("pe", L[("cbmm", (e, t))])
            else:
                scale_lm = ("pe", L[("gbmm", t)])

            slots1 = {}
            slots2 = {}
            for fc in range(2):
                slots1[fc] = pool_load(w1(fc))
                slots2[fc] = pool_load(w2(fc))

            for fc in range(FCM):
                if fc + 2 < FCM:
                    slots1[fc + 2] = pool_load(w1(fc + 2))
                    slots2[fc + 2] = pool_load(w2(fc + 2))

                def f_h12(tt, bi=bi, fc=fc, t=t):
                    s1, wv1 = slots1[fc]
                    s2, wv2 = slots2[fc]
                    E.w(tt, f"wl{s1}", wv1)
                    E.w(tt, "act", L.get(("sil", (bi, fc - 1, t))))
                    if fc == 0:
                        if bi == 0:
                            E.w(tt, "dve", L[("cfm", t)])
                        else:
                            E.w(tt, "dve", L[("yadd", (bi - 1, NCH - 1, t))])
                    ins = None
                    for n in range(2):
                        for c in range(NCH):
                            ins = tt.matmul(
                                pq[0].ap()[:, n * 512:(n + 1) * 512],
                                pool[s1].ap()[:, c * 128:(c + 1) * 128],
                                xn_t.ap()[:, c * TILE + n * 512:
                                          c * TILE + (n + 1) * 512],
                                start=(c == 0), stop=(c == NCH - 1))
                    L[("h1", (bi, fc, t))] = E.inc(ins, "pe")
                    pool_done(s1)
                    E.w(tt, f"wl{s2}", wv2)
                    E.w(tt, "dve", L.get(("hmul", (bi, fc - 1, t))))
                    if fc == 0 and bi == 0:
                        E.w(tt, "act", L[("sig", t)])
                    for n in range(2):
                        for c in range(NCH):
                            ins = tt.matmul(
                                pq[1].ap()[:, n * 512:(n + 1) * 512],
                                pool[s2].ap()[:, c * 128:(c + 1) * 128],
                                xn_t.ap()[:, c * TILE + n * 512:
                                          c * TILE + (n + 1) * 512],
                                start=(c == 0), stop=(c == NCH - 1))
                    L[("h2", (bi, fc, t))] = E.inc(ins, "pe")
                    pool_done(s2)
                    if bi == NBLK - 1 and fc == FCM - 1:
                        L[("xnfree", t)] = L[("h2", (bi, fc, t))]

                block.tensor(f_h12)

                def f_sil(sc, bi=bi, fc=fc, t=t):
                    E.w(sc, "pe", L[("h1", (bi, fc, t))])
                    E.w(sc, "dve", L.get(("hmul", (bi, fc - 2, t))))
                    ins = sc.activation(silb[fc % 2].ap(), pq[0].ap(),
                                        AF.Silu)
                    L[("sil", (bi, fc, t))] = E.inc(ins, "act")

                block.scalar(f_sil)

                def f_hmul(v, bi=bi, fc=fc, t=t, scale_lm=scale_lm):
                    hv = aoh.ap()[:, fc * TILE:(fc + 1) * TILE]
                    E.w(v, "act", L[("sil", (bi, fc, t))])
                    E.w(v, "pe", L[("h2", (bi, fc, t))])
                    if fc == 0:
                        if bi == 0:
                            E.w(v, "pe", L[("womm", (NCH - 1, t))])
                        else:
                            E.w(v, "pe", L[("projlast", (bi - 1, t))])
                    ins = v.tensor_mul(hv, silb[fc % 2].ap(), pq[1].ap())
                    L[("hmul", (bi, fc, t))] = E.inc(ins, "dve")
                    E.w(v, "dve", E.C["dve"])
                    E.w(v, scale_lm[0], scale_lm[1])
                    ins = v.tensor_mul(hv, hv, pq[2].ap())
                    L[("hsc", (bi, fc, t))] = E.inc(ins, "dve")

                block.vector(f_hmul)

            pslots = {}
            for cc in range(2):
                pslots[cc] = pool_load(wp(cc))
            for cc in range(NCH):
                if cc + 2 < NCH:
                    pslots[cc + 2] = pool_load(wp(cc + 2))

                def f_proj(tt, bi=bi, cc=cc, t=t):
                    s, wv = pslots[cc]
                    E.w(tt, f"wl{s}", wv)
                    bank = pq[3 if cc % 2 == 0 else 0].ap()
                    E.w(tt, "dve", L.get(("yadd", (bi, cc - 2, t))))
                    if cc == 0:
                        if bi == 0:
                            E.w(tt, "dve", L[("ao", (NHEAD - 1, t))])
                        else:
                            E.w(tt, "dve", L[("yadd", (bi - 1, NCH - 1, t))])
                    if cc == 1:
                        E.w(tt, "act", L[("sil", (bi, FCM - 1, t))])
                    ins = None
                    for fc in range(FCM):
                        if cc == 0:
                            E.w(tt, "dve", L[("hsc", (bi, fc, t))])
                        for n in range(2):
                            ins = tt.matmul(
                                bank[:, n * 512:(n + 1) * 512],
                                pool[s].ap()[:, fc * 128:(fc + 1) * 128],
                                aoh.ap()[:, fc * TILE + n * 512:
                                         fc * TILE + (n + 1) * 512],
                                start=(fc == 0), stop=(fc == FCM - 1))
                    L[("projmm", (cc, t))] = E.inc(ins, "pe")
                    if cc == NCH - 1:
                        L[("projlast", (bi, t))] = E.C["pe"]
                    pool_done(s)

                block.tensor(f_proj)

                def f_yadd(v, bi=bi, cc=cc, t=t):
                    bank = pq[3 if cc % 2 == 0 else 0].ap()
                    E.w(v, "pe", L[("projmm", (cc, t))])
                    E.w(v, "dve", L[("woev", (cc, t))])
                    ya = y_acc.ap()[:, cc * TILE:(cc + 1) * TILE]
                    ins = v.tensor_add(ya, ya, bank)
                    L[("yadd", (bi, cc, t))] = E.inc(ins, "dve")

                block.vector(f_yadd)

        for bi in range(NBLK):
            emit_block(bi, t)

        for cc in range(NCH):
            p = cc % 2

            def f_xld(s, cc=cc, t=t, p=p, T0=T0):
                if t == 0 and cc < 2:
                    E.w(s, "pe", R["passa_pe"])
                E.w(s, "dve", prev_yev[p])
                E.w(s, prev_store_sem[p], prev_store[p])
                E.inc(s.dma_start(
                    xev[p].ap().rearrange("p (a u) -> p a u", a=1),
                    R["x_r"][:, cc:cc + 1, T0:T0 + TILE]), f"sxe{p}", 16)

            block.sync(f_xld)
            xe_ready = E.C[f"sxe{p}"]

            def f_yev(v, cc=cc, t=t, p=p, xe_ready=xe_ready):
                E.w(v, f"sxe{p}", xe_ready)
                E.w(v, prev_store_sem[p], prev_store[p])
                E.w(v, "dve", L[("yadd", (LAST, cc, t))])
                ya = y_acc.ap()[:, cc * TILE:(cc + 1) * TILE]
                ins = v.tensor_add(xev[p].ap(), xev[p].ap(), ya)
                L[("yev", (cc, t))] = E.inc(ins, "dve")

            block.vector(f_yev)
            prev_yev[p] = L[("yev", (cc, t))]

            gp_store = (t == NTILE - 1 and p == 1)
            sem_st = "sstg" if gp_store else f"sst{p}"

            def f_st(s, cc=cc, t=t, p=p, T0=T0, sem_st=sem_st):
                E.w(s, "dve", L[("yev", (cc, t))])
                E.inc(s.dma_start(
                    R["y_r"][:, cc:cc + 1, T0:T0 + TILE],
                    xev[p].ap().rearrange("p (a u) -> p a u", a=1)),
                    sem_st, 16)

            if gp_store:
                block.gpsimd(f_st)
            else:
                block.sync(f_st)
            prev_store[p] = E.C[sem_st]
            prev_store_sem[p] = sem_st
            L[("yfree", (cc, t))] = E.C[sem_st]

    def f_fin(s):
        E.w(s, "sst0", E.C.get("sst0", 0))
        E.w(s, "sst1", E.C.get("sst1", 0))
        E.w(s, "sstg", E.C.get("sstg", 0))

    block.sync(f_fin)


_PROG_CACHE = {}


def _get_program():
    key = (DO_PASSB, DBG_TAPS, DEV_HALVES, DEV_CHUNKS, DEV_CC, DEV_KV_COPY)
    if key not in _PROG_CACHE:
        _PROG_CACHE[key] = build_core_program()
    return _PROG_CACHE[key]


LAST_RESULTS = None


def host_inputs(L0_attn_x, L0_attn_Wq, L0_attn_bq, L0_attn_Wk, L0_attn_bk,
                L0_attn_Wv, L0_attn_bv, L0_attn_Wo, L0_attn_bo,
                L0_ffn_gate_w, L0_ffn_expert_fc1, L0_ffn_expert_fc2,
                L0_ffn_expert_proj, L0_ffn_shared_fc1, L0_ffn_shared_fc2,
                L0_ffn_shared_proj, L0_ffn_shared_gate_w):
    f32 = np.float32

    def cT(a):
        return np.ascontiguousarray(np.asarray(a, dtype=f32).T)

    wqT = cT(L0_attn_Wq)
    wkT = cT(L0_attn_Wk)
    wvT = cT(L0_attn_Wv)
    woT = cT(L0_attn_Wo)
    bq_cols = np.ascontiguousarray(
        np.asarray(L0_attn_bq, f32).reshape(NCH, 128).T)
    bo_cols = np.ascontiguousarray(
        np.asarray(L0_attn_bo, f32).reshape(NCH, 128).T)
    bkv_rows = np.ascontiguousarray(
        np.concatenate([np.asarray(L0_attn_bk, f32),
                        np.asarray(L0_attn_bv, f32)])[None, :])
    gw = np.asarray(L0_ffn_gate_w, f32)
    gate_cols = np.ascontiguousarray(
        gw.T.reshape(NCH, 128, NEXP).transpose(1, 0, 2).reshape(128, NCH * NEXP))
    sg_cols = np.ascontiguousarray(
        np.asarray(L0_ffn_shared_gate_w, f32).reshape(NCH, 128).T)
    fc1T = np.ascontiguousarray(
        np.asarray(L0_ffn_expert_fc1, f32).transpose(0, 2, 1))
    fc2T = np.ascontiguousarray(
        np.asarray(L0_ffn_expert_fc2, f32).transpose(0, 2, 1))
    projT = np.ascontiguousarray(
        np.asarray(L0_ffn_expert_proj, f32).transpose(0, 2, 1))
    sfc1T = cT(L0_ffn_shared_fc1)
    sfc2T = cT(L0_ffn_shared_fc2)
    sprojT = cT(L0_ffn_shared_proj)
    ones_col = np.ones((128, 1), f32)
    ones_row_d = np.ones((1, 128), f32)
    ident_d = np.eye(128, dtype=f32)
    oh8_d = np.zeros((8, NEXP * 128), f32)
    for e in range(NEXP):
        oh8_d[e, e * 128:(e + 1) * 128] = 1.0

    x = np.asarray(L0_attn_x, f32)
    in_maps = []
    for core in range(8):
        b = core // 4
        p0 = (core % 4) * TCORE
        x_fm_a = np.ascontiguousarray(x[b, p0:p0 + TCORE, :].T)
        pos = (p0 + np.arange(TCORE, dtype=np.int64)).astype(f32)
        pos_row_a = np.ascontiguousarray(pos[None, :])
        pos_cols_a = np.ascontiguousarray(pos.reshape(NCHUNK, 128).T)
        in_maps.append(dict(
            x_fm=x_fm_a, pos_row=pos_row_a, pos_cols=pos_cols_a,
            wqT=wqT, wkT=wkT, wvT=wvT, woT=woT,
            bq_cols=bq_cols, bo_cols=bo_cols, bkv_rows=bkv_rows,
            gate_cols=gate_cols, sg_cols=sg_cols,
            fc1T=fc1T, fc2T=fc2T, projT=projT,
            sfc1T=sfc1T, sfc2T=sfc2T, sprojT=sprojT,
            ones_col=ones_col, ones_row_d=ones_row_d, ident_d=ident_d,
            oh8_d=oh8_d,
        ))
    return in_maps


def kernel(**inputs):
    global LAST_RESULTS
    in_maps = host_inputs(**inputs)
    nc = _get_program()
    res = run_bass_kernel_spmd(nc, in_maps[:DEV_CORES],
                               core_ids=list(range(DEV_CORES)))
    LAST_RESULTS = res
    if DEV_CORES < 8:
        return np.zeros((BATCH, SEQ, D), np.float32)

    out = np.empty((BATCH, SEQ, D), dtype=np.float32)
    for core in range(8):
        b = core // 4
        p0 = (core % 4) * TCORE
        out[b, p0:p0 + TCORE, :] = res.results[core]["y_fm"].T
    return out


# version 09 (final)
# speedup vs baseline: 1.0412x; 1.0412x over previous
import numpy as np
import concourse.bass as bass
import concourse.mybir as mybir
from concourse.bass_utils import run_bass_kernel_spmd

F32 = mybir.dt.float32
BF16 = mybir.dt.bfloat16
AF = mybir.ActivationFunctionType
ALU = mybir.AluOpType
AX = mybir.AxisListType

D = 2048
NCH = 16
SEQ = 16384
BATCH = 2
TCORE = 4096
NCHUNK = 32
NHEAD = 16
NEXP = 8
FMOE = 1408
FCM = 11
FSH = 5632
NSHBLK = 4
NBLK = NSHBLK + NEXP
TILE = 1024
NTILE = 4
LN_EPS = 1e-5
Z_EPS = 1e-6
KVG = 8
POOL_N = 8

DO_PASSB = True
DBG_TAPS = False
DEV_HALVES = 2
DEV_CHUNKS = NCHUNK
DEV_CC = True
DEV_CORES = 8
DEV_KV_COPY = False


class Em:
    def __init__(self, nc):
        self.nc = nc
        self.S = {}
        self.C = {}

    def sem(self, name):
        if name not in self.S:
            self.S[name] = self.nc.alloc_semaphore(name)
            self.C[name] = 0
        return self.S[name]

    def inc(self, ins, name, n=1):
        ins.then_inc(self.sem(name), n)
        self.C[name] += n
        return self.C[name]

    def w(self, eng, name, val):
        if val:
            eng.wait_ge(self.sem(name), val)


def build_core_program():
    nc = bass.Bass("TRN2", target_bir_lowering=False)

    def din(name, shape, dt=F32):
        return nc.dram_tensor(name, list(shape), dt, kind="ExternalInput").ap()

    x_fm = din("x_fm", (D, TCORE))
    pos_row = din("pos_row", (1, TCORE))
    pos_cols = din("pos_cols", (128, NCHUNK))
    wqT = din("wqT", (D, D))
    wkT = din("wkT", (D, D))
    wvT = din("wvT", (D, D))
    woT = din("woT", (D, D))
    bq_cols = din("bq_cols", (128, NCH))
    bo_cols = din("bo_cols", (128, NCH))
    bkv_rows = din("bkv_rows", (1, 2 * D))
    gate_cols = din("gate_cols", (128, NCH * NEXP))
    sg_cols = din("sg_cols", (128, NCH))
    fc1T = din("fc1T", (NEXP, D, FMOE))
    fc2T = din("fc2T", (NEXP, D, FMOE))
    projT = din("projT", (NEXP, FMOE, D))
    sfc1T = din("sfc1T", (D, FSH))
    sfc2T = din("sfc2T", (D, FSH))
    sprojT = din("sprojT", (FSH, D))
    ones_col_d = din("ones_col", (128, 1))
    ones_row_d = din("ones_row_d", (1, 128))
    ident_d = din("ident_d", (128, 128))
    oh8_d = din("oh8_d", (8, NEXP * 128))

    y_fm = nc.dram_tensor("y_fm", [D, TCORE], F32, kind="ExternalOutput").ap()
    xn_kind = "ExternalOutput" if DBG_TAPS else "Internal"
    xn_scr = nc.dram_tensor("xn_scr", [D, TCORE], BF16, kind=xn_kind).ap()
    cc_in = nc.dram_tensor("cc_in", [128, NHEAD * 132], F32, kind="Internal").ap()
    cc_out = nc.dram_tensor("cc_out", [128, NHEAD * 132], F32,
                            kind="Internal").ap()
    kv_tap = None
    if DBG_TAPS:
        kv_tap = nc.dram_tensor("kv_tap", [128, NHEAD * 132], F32,
                                kind="ExternalOutput").ap()

    x_r = x_fm.rearrange("(c p) t -> p c t", p=128)
    xn_r = xn_scr.rearrange("(c p) t -> p c t", p=128)
    y_r = y_fm.rearrange("(c p) t -> p c t", p=128)
    wq_r = wqT.rearrange("(c p) d -> p c d", p=128)
    wk_r = wkT.rearrange("(c p) f -> p c f", p=128)
    wv_r = wvT.rearrange("(c p) f -> p c f", p=128)
    wo_r = woT.rearrange("(c p) d -> p c d", p=128)
    fc1_r = fc1T.rearrange("e (c p) f -> e p c f", p=128)
    fc2_r = fc2T.rearrange("e (c p) f -> e p c f", p=128)
    proj_r = projT.rearrange("e (fc p) c -> e p fc c", p=128)
    sfc1_r = sfc1T.rearrange("(c p) f -> p c f", p=128)
    sfc2_r = sfc2T.rearrange("(c p) f -> p c f", p=128)
    sproj_r = sprojT.rearrange("(fc p) c -> p fc c", p=128)

    A = nc.alloc_sbuf_tensor
    pool = [A(f"pool{i}", [128, 2048], BF16) for i in range(POOL_N)]
    kv_bf = A("kv_bf", [128, NHEAD * 132], BF16)
    biasq = A("biasq", [128, NCH], F32)
    biaso = A("biaso", [128, NCH], F32)
    dcols = A("dcols", [128, NCHUNK], F32)
    pcols = A("pcols", [128, NCHUNK], F32)
    gate_b = A("gate_b", [128, NCH * NEXP], BF16)
    sg_b = A("sg_b", [128, NCH], BF16)
    onec = A("onec", [128, 1], BF16)
    identb = A("identb", [128, 128], BF16)
    oh8 = A("oh8", [8, NEXP * 128], BF16)
    rows_b = A("rows_b", [128, 4224], BF16)
    rows_f = A("rows_f", [1, 1664], F32)

    P = nc.alloc_psum_tensor
    pq = [P(f"pq{i}", [128, 1024], F32) for i in range(4)]
    pq_bf = [t.ap().bitcast(BF16) for t in pq]

    xn_t_h = A("xn_t", [128, NCH * TILE], BF16)

    E = Em(nc)
    L = {}
    onesr = rows_b.ap()[0:1, 0:128]

    with nc.Block() as block:
        pool_state = dict(fill=[0] * POOL_N, free=[0] * POOL_N, rr=[0])

        def pool_load(src_ap):
            s = pool_state["rr"][0]
            pool_state["rr"][0] = (s + 1) % POOL_N
            need = pool_state["free"][s]
            a, b = src_ap.shape[1], src_ap.shape[2]
            dst = pool[s].ap()[:, 0:a * b].rearrange("p (a b) -> p a b", a=a)

            def fg(g, s=s, need=need, src_ap=src_ap, dst=dst):
                E.w(g, "pe", need)
                E.inc(g.dma_start(dst, src_ap), f"wl{s}", 16)

            block.gpsimd(fg)
            pool_state["fill"][s] += 1
            return s, 16 * pool_state["fill"][s]

        def pool_done(s):
            pool_state["free"][s] = E.C["pe"]

        def setup_sync(s):
            for dst, src in ((biasq.ap(), bq_cols), (biaso.ap(), bo_cols),
                             (pcols.ap(), pos_cols)):
                E.inc(s.dma_start(dst, src), "set", 16)

        def setup_gp(g):
            for dst, src in (
                (gate_b.ap(), gate_cols),
                (sg_b.ap(), sg_cols),
                (onec.ap(), ones_col_d),
                (rows_b.ap()[0:1, 0:128], ones_row_d),
                (identb.ap(), ident_d),
                (oh8.ap(), oh8_d),
            ):
                E.inc(g.dma_start(dst, src), "setg", 16)

        block.sync(setup_sync)
        block.gpsimd(setup_gp)
        set_done = E.C["set"]
        setg_done = E.C["setg"]

        def setup_act(sc):
            E.w(sc, "set", set_done)
            E.w(sc, "setg", setg_done)
            ins = sc.activation(dcols.ap(), pcols.ap(), AF.Exp,
                                scale=-1.0 / SEQ)
            L["dcols"] = E.inc(ins, "act")

        block.scalar(setup_act)

        with (
            nc.sbuf_tensor("wkh", [128, NCH * 1024], BF16) as wkh,
            nc.sbuf_tensor("wvh", [128, NCH * 1024], BF16) as wvh,
            nc.sbuf_tensor("xb0", [128, D], BF16) as xb0,
            nc.sbuf_tensor("xb1", [128, D], BF16) as xb1,
            nc.sbuf_tensor("xq0", [128, D], BF16) as xq0,
            nc.sbuf_tensor("xq1", [128, D], BF16) as xq1,
            nc.sbuf_tensor("xn0", [128, D], BF16) as xn0,
            nc.sbuf_tensor("xn1", [128, D], BF16) as xn1,
            nc.sbuf_tensor("tmpa", [128, D], BF16) as tmpa,
            nc.sbuf_tensor("ksb0", [128, 1024], BF16) as ksb0,
            nc.sbuf_tensor("ksb1", [128, 1024], BF16) as ksb1,
            nc.sbuf_tensor("vsb0", [128, 1024], BF16) as vsb0,
            nc.sbuf_tensor("vsb1", [128, 1024], BF16) as vsb1,
            nc.sbuf_tensor("kv_sb", [128, NHEAD * 132], F32) as kv_sb,
            nc.sbuf_tensor("bkv_b", [1, 2 * D], BF16) as bkv_b,
            nc.sbuf_tensor("mrbc", [128, 256], BF16) as mrbc,
        ):
            xb = [xb0, xb1]
            xq = [xq0, xq1]
            xn2 = [xn0, xn1]
            ksb = [ksb0, ksb1]
            vsb = [vsb0, vsb1]
            kv_sb3 = kv_sb.ap().rearrange("p (h w) -> p h w", h=NHEAD)

            st_sx = pq[0].ap()[0:1, 0:128]
            st_sq = pq[0].ap()[0:1, 128:256]
            m_bc = pq[0].ap()[:, 512:640]
            r_bc = pq[0].ap()[:, 640:768]
            ks_p = pq[0].ap()[:, 768:784]
            k_ps = pq[1].ap()
            v_ps = pq[2].ap()
            kv_ps = pq[3].ap()

            RF = rows_f.ap()
            r_m = RF[0:1, 0:128]
            r_msq = RF[0:1, 128:256]
            r_var = RF[0:1, 256:384]
            r_lnv = RF[0:1, 384:512]
            r_rstd = RF[0:1, 512:640]

            def xload(i):
                p = i % 2
                srcx = x_r[:, :, i * 128:(i + 1) * 128]

                def fg(g, i=i, p=p, srcx=srcx):
                    E.w(g, "pe", L.get(("stats", i - 2)))
                    E.w(g, "dve", L.get(("xsq", i - 2)))
                    E.w(g, "dve", L.get(("xnsub", i - 2)))
                    E.inc(g.dma_start(
                        xb[p].ap().rearrange("p (c t) -> p c t", c=NCH),
                        srcx), f"sxb{p}", 16)

                block.gpsimd(fg)

            def xnload(i):
                p = i % 2

                def fs(s, i=i, p=p):
                    E.w(s, "pe", L.get(("kvproj", i - 2)))
                    L[("xnld", i)] = E.inc(s.dma_start(
                        xn2[p].ap().rearrange("p (c t) -> p c t", c=NCH),
                        xn_r[:, :, i * 128:(i + 1) * 128]), f"sxn{p}", 16)

                block.sync(fs)

            xload(0)
            xload(1)
            for i in range(DEV_CHUNKS):
                pi = i % 2

                def f_xsq(v, i=i, pi=pi):
                    E.w(v, f"sxb{pi}", 16 * (i // 2 + 1))
                    E.w(v, "pe", L.get(("stats", i - 2)))
                    ins = v.tensor_mul(xq[pi].ap(), xb[pi].ap(),
                                       xb[pi].ap())
                    L[("xsq", i)] = E.inc(ins, "dve")

                block.vector(f_xsq)

                def f_stats(t, i=i, pi=pi):
                    E.w(t, f"sxb{pi}", 16 * (i // 2 + 1))
                    E.w(t, "set", set_done)
                    E.w(t, "setg", setg_done)
                    E.w(t, "dve", L.get(("var", i - 1)))
                    for c in range(NCH):
                        t.matmul(st_sx, onec.ap(),
                                 xb[pi].ap()[:, c * 128:(c + 1) * 128],
                                 start=(c == 0), stop=(c == NCH - 1))
                    E.w(t, "dve", L[("xsq", i)])
                    ins = None
                    for c in range(NCH):
                        ins = t.matmul(
                            st_sq, onec.ap(),
                            xq[pi].ap()[:, c * 128:(c + 1) * 128],
                            start=(c == 0), stop=(c == NCH - 1))
                    L[("stats", i)] = E.inc(ins, "pe")

                block.tensor(f_stats)

                def f_rows1(v, i=i):
                    E.w(v, "pe", L[("stats", i)])
                    ins = v.tensor_scalar_mul(r_m, st_sx, 1.0 / D)
                    E.inc(ins, "dve")
                    E.w(v, "dve", E.C["dve"])
                    ins = v.tensor_mul(r_msq, r_m, r_m)
                    E.inc(ins, "dve")
                    E.w(v, "dve", E.C["dve"])
                    ins = v.scalar_tensor_tensor(
                        r_var, st_sq, 1.0 / D, r_msq,
                        op0=ALU.mult, op1=ALU.subtract)
                    E.inc(ins, "dve")
                    E.w(v, "dve", E.C["dve"])
                    ins = v.tensor_scalar_add(r_var, r_var, LN_EPS)
                    L[("var", i)] = E.inc(ins, "dve")

                block.vector(f_rows1)

                def f_rstd(sc, i=i):
                    E.w(sc, "dve", L[("var", i)])
                    ins = sc.activation(r_lnv, r_var, AF.Ln)
                    E.inc(ins, "act")
                    E.w(sc, "act", E.C["act"])
                    ins = sc.activation(r_rstd, r_lnv, AF.Exp, scale=-0.5)
                    L[("rstd", i)] = E.inc(ins, "act")

                block.scalar(f_rstd)

                def f_rows2(v, i=i, pi=pi):
                    mb = rows_b.ap()[0:1, 128 + pi * 128:256 + pi * 128]
                    rb = rows_b.ap()[0:1, 384 + pi * 128:512 + pi * 128]
                    E.w(v, "pe", L.get(("bcast", i - 2)))
                    E.w(v, "act", L[("rstd", i)])
                    ins = v.tensor_copy(mb, r_m)
                    E.inc(ins, "dve")
                    ins = v.tensor_copy(rb, r_rstd)
                    L[("rowsbf", i)] = E.inc(ins, "dve")

                block.vector(f_rows2)

                def f_warm(t, i=i):
                    for k in range(3):
                        E.w(t, "dve", L[("xsq", i)])
                        t.matmul(kv_ps[0:64, 0:64], onesr[0:1, 0:64],
                                 onesr[0:1, 0:64], start=True, stop=True)

                block.tensor(f_warm)

                def f_bcast(t, i=i, pi=pi):
                    E.w(t, "act", L.get(("mrc", i - 1)))
                    E.w(t, "dve", L[("rowsbf", i)])
                    mb = rows_b.ap()[0:1, 128 + pi * 128:256 + pi * 128]
                    rb = rows_b.ap()[0:1, 384 + pi * 128:512 + pi * 128]
                    t.matmul(m_bc, onesr, mb, start=True, stop=True)
                    ins = t.matmul(r_bc, onesr, rb, start=True, stop=True)
                    L[("bcast", i)] = E.inc(ins, "pe")

                block.tensor(f_bcast)

                def f_mrc(sc, i=i):
                    E.w(sc, "pe", L[("bcast", i)])
                    E.w(sc, "dve", L.get(("xnmul", i - 1)))
                    ins = sc.copy(mrbc.ap()[:, 0:128], m_bc)
                    E.inc(ins, "act")
                    ins = sc.copy(mrbc.ap()[:, 128:256], r_bc)
                    L[("mrc", i)] = E.inc(ins, "act")

                block.scalar(f_mrc)

                def f_xn(v, i=i, pi=pi):
                    E.w(v, "act", L[("mrc", i)])
                    E.w(v, f"sspill{pi}", 16 * (i // 2))
                    xba = xb[pi].ap().rearrange("p (c t) -> p c t", c=NCH)
                    mb3 = mrbc.ap()[:, 0:128].unsqueeze(1).broadcast_to(
                        (128, NCH, 128))
                    rb3 = mrbc.ap()[:, 128:256].unsqueeze(1).broadcast_to(
                        (128, NCH, 128))
                    tm3 = tmpa.ap().rearrange("p (c t) -> p c t", c=NCH)
                    xn3 = xn2[pi].ap().rearrange("p (c t) -> p c t", c=NCH)
                    with nc.allow_low_precision(
                            reason="xn is a bf16 matmul operand"):
                        ins = v.tensor_sub(tm3, xba, mb3)
                        L[("xnsub", i)] = E.inc(ins, "dve")
                        E.w(v, "dve", E.C["dve"])
                        ins = v.tensor_mul(xn3, tm3, rb3)
                    L[("xnmul", i)] = E.inc(ins, "dve")

                block.vector(f_xn)

                def f_spill(s, i=i, pi=pi):
                    E.w(s, "dve", L[("xnmul", i)])
                    E.inc(s.dma_start(
                        xn_r[:, :, i * 128:(i + 1) * 128],
                        xn2[pi].ap().rearrange("p (c t) -> p c t", c=NCH)),
                        f"sspill{pi}", 16)

                block.sync(f_spill)

                if i + 2 < DEV_CHUNKS:
                    xload(i + 2)

            for half in range(DEV_HALVES):
                HOFF = half * 1024

                bar = {k: E.C.get(k, 0)
                       for k in ("pe", "act", "dve", "sspill0", "sspill1")}

                def f_bar_t(tt, bar=bar):
                    E.w(tt, "act", bar["act"])
                    E.w(tt, "dve", bar["dve"])

                def f_bar_v(v, bar=bar):
                    E.w(v, "pe", bar["pe"])
                    E.w(v, "act", bar["act"])

                def f_bar_a(sc, bar=bar):
                    E.w(sc, "pe", bar["pe"])
                    E.w(sc, "dve", bar["dve"])

                def f_bar_s(s, bar=bar):
                    E.w(s, "pe", bar["pe"])
                    E.w(s, "dve", bar["dve"])
                    E.w(s, "sspill0", bar["sspill0"])
                    E.w(s, "sspill1", bar["sspill1"])

                block.tensor(f_bar_t)
                block.vector(f_bar_v)
                block.scalar(f_bar_a)
                block.sync(f_bar_s)
                Lclear = [k for k in L if isinstance(k, tuple)]
                for k in Lclear:
                    del L[k]

                def wload(g, half=half, HOFF=HOFF):
                    if half == 0:
                        ins = g.memset(kv_sb.ap(), 0.0)
                        E.inc(ins, "kvz", 1)
                        E.inc(g.dma_start(bkv_b.ap(), bkv_rows), "swk", 16)
                    if half == 1:
                        E.w(g, "pe", E.C["pe"])
                    E.inc(g.dma_start(
                        wkh.ap().rearrange("p (c f) -> p c f", c=NCH),
                        wk_r[:, :, HOFF:HOFF + 1024]), "swk", 16)
                    E.inc(g.dma_start(
                        wvh.ap().rearrange("p (c f) -> p c f", c=NCH),
                        wv_r[:, :, HOFF:HOFF + 1024]), "swk", 16)

                block.gpsimd(wload)
                wk_done = E.C["swk"]

                xnload(0)
                xnload(1)

                for i in range(DEV_CHUNKS + 1):
                    j = i
                    jp = i - 1
                    pj = j % 2

                    if j < DEV_CHUNKS:
                        def f_kv(t, j=j, pj=pj, half=half):
                            E.w(t, "swk", wk_done)
                            xnb = xn2[pj]
                            E.w(t, f"sxn{pj}", L[("xnld", j)])
                            E.w(t, "act", L.get(("kevac", j - 1)))
                            for q in range(2):
                                for c in range(NCH):
                                    t.matmul(
                                        k_ps[:, q * 512:(q + 1) * 512],
                                        xnb.ap()[:, c * 128:(c + 1) * 128],
                                        wkh.ap()[:, c * 1024 + q * 512:
                                                 c * 1024 + (q + 1) * 512],
                                        start=(c == 0), stop=False)
                                t.matmul(
                                    k_ps[:, q * 512:(q + 1) * 512],
                                    onesr,
                                    bkv_b.ap()[0:1, half * 1024 + q * 512:
                                               half * 1024 + (q + 1) * 512],
                                    start=False, stop=True)
                            E.w(t, "act", L.get(("vevac", j - 1)))
                            ins = None
                            for q in range(2):
                                for c in range(NCH):
                                    t.matmul(
                                        v_ps[:, q * 512:(q + 1) * 512],
                                        xnb.ap()[:, c * 128:(c + 1) * 128],
                                        wvh.ap()[:, c * 1024 + q * 512:
                                                 c * 1024 + (q + 1) * 512],
                                        start=(c == 0), stop=False)
                                ins = t.matmul(
                                    v_ps[:, q * 512:(q + 1) * 512],
                                    onesr,
                                    bkv_b.ap()[0:1, 2048 + half * 1024 +
                                               q * 512:
                                               2048 + half * 1024 +
                                               (q + 1) * 512],
                                    start=False, stop=True)
                            L[("kvproj", j)] = E.inc(ins, "pe")

                        block.tensor(f_kv)

                        def f_kvev(sc, j=j, pj=pj):
                            E.w(sc, "pe", L[("kvproj", j)])
                            E.w(sc, "act", L["dcols"])
                            E.w(sc, "pe", L.get(("kvmm", j - 2)))
                            ins = sc.activation(ksb[pj].ap(), k_ps, AF.Relu,
                                                scale=dcols.ap()[:, j:j + 1])
                            L[("kevac", j)] = E.inc(ins, "act")
                            ins = sc.copy(vsb[pj].ap(), v_ps)
                            L[("vevac", j)] = E.inc(ins, "act")

                        block.scalar(f_kvev)

                        if j + 2 < DEV_CHUNKS:
                            xnload(j + 2)

                    if 0 <= jp < DEV_CHUNKS:
                        def f_pe2(t, j=jp, pj=jp % 2):
                            E.w(t, "act", L[("kevac", j)])
                            E.w(t, "act", L[("vevac", j)])
                            E.w(t, "dve", L.get(("kvev", j - 1)))
                            ins = None
                            for hh in range(8):
                                ins = t.matmul(
                                    ks_p[:, hh:hh + 1],
                                    ksb[pj].ap()[:, hh * 128:(hh + 1) * 128],
                                    onec.ap(),
                                    start=True, stop=True)
                            for hh in range(8):
                                ins = t.matmul(
                                    kv_ps[:, hh * 128:(hh + 1) * 128],
                                    ksb[pj].ap()[:, hh * 128:(hh + 1) * 128],
                                    vsb[pj].ap()[:, hh * 128:(hh + 1) * 128],
                                    start=True, stop=True)
                            L[("kvmm", j)] = E.inc(ins, "pe")

                        block.tensor(f_pe2)

                        def f_dve2(v, j=jp, half=half):
                            E.w(v, "pe", L[("kvmm", j)])
                            dstks = kv_sb3[:, half * 8:half * 8 + 8, 128:129]
                            srcks = ks_p[:, 0:8].rearrange(
                                "p (h w) -> p h w", w=1)
                            dstkv = kv_sb3[:, half * 8:half * 8 + 8, 0:128]
                            srckv = kv_ps.rearrange("p (h w) -> p h w", h=8)
                            if j == 0:
                                E.w(v, "kvz", 1)
                                ins = v.tensor_copy(dstkv, srckv)
                            else:
                                ins = v.tensor_add(dstkv, dstkv, srckv)
                            E.inc(ins, "dve")
                            E.w(v, "dve", E.C["dve"])
                            if j == 0:
                                ins = v.tensor_copy(dstks, srcks)
                            else:
                                ins = v.tensor_add(dstks, dstks, srcks)
                            L[("kvev", j)] = E.inc(ins, "dve")

                        block.vector(f_dve2)

            if DO_PASSB:
                def f_xn0a(g):
                    E.w(g, "sspill0", E.C.get("sspill0", 0))
                    E.w(g, "sspill1", E.C.get("sspill1", 0))
                    E.inc(g.dma_start(
                        xn_t_h.ap()[:, 0:12 * TILE].rearrange(
                            "p (c u) -> p c u", c=12),
                        xn_r[:, 0:12, 0:TILE]), "sxnB", 16)

                block.gpsimd(f_xn0a)

                def f_xn0b(s):
                    E.w(s, "sspill0", E.C.get("sspill0", 0))
                    E.w(s, "sspill1", E.C.get("sspill1", 0))
                    E.inc(s.dma_start(
                        xn_t_h.ap()[:, 12 * TILE:].rearrange(
                            "p (c u) -> p c u", c=4),
                        xn_r[:, 12:16, 0:TILE]), "sxnC", 16)

                block.sync(f_xn0b)

            def f_cc(g):
                E.w(g, "dve", L[("kvev", DEV_CHUNKS - 1)])
                if not DEV_CC:
                    E.inc(g.dma_start(kv_bf.ap(), kv_sb.ap()), "sccd", 16)
                    if DBG_TAPS:
                        E.inc(g.dma_start(kv_tap, kv_sb.ap()), "sccd", 16)
                    return
                E.inc(g.dma_start(cc_in, kv_sb.ap()), "sccd", 16)
                E.w(g, "sccd", E.C["sccd"])
                ins = g.collective_compute(
                    "AllReduce", ALU.add,
                    replica_groups=[[0, 1, 2, 3], [4, 5, 6, 7]],
                    ins=[cc_in.opt()],
                    outs=[cc_out.opt()],
                )
                E.inc(ins, "sccc", 1)
                E.w(g, "sccc", E.C["sccc"])
                E.inc(g.dma_start(kv_bf.ap(), cc_out), "sccd", 16)
                if DBG_TAPS:
                    E.inc(g.dma_start(kv_tap, cc_out), "sccd", 16)

            wq_pre = {}
            if DO_PASSB:
                for h in range(POOL_N):
                    wq_pre[h] = pool_load(
                        wq_r[:, :, h * 128:(h + 1) * 128])

            block.gpsimd(f_cc)
            cc_done = E.C["sccd"]
            passa_pe = E.C["pe"]
            passa_dve = E.C["dve"]

        if DO_PASSB:
            _emit_pass_b(nc, block, E, L, dict(
                x_r=x_r, xn_r=xn_r, y_r=y_r, wq_r=wq_r, wo_r=wo_r,
                gate_b=gate_b, sg_b=sg_b, fc1_r=fc1_r, fc2_r=fc2_r,
                proj_r=proj_r, sfc1_r=sfc1_r, sfc2_r=sfc2_r, sproj_r=sproj_r,
                pos_row=pos_row, pool=pool, kv_bf=kv_bf, biasq=biasq,
                biaso=biaso, rows_b=rows_b, rows_f=rows_f, identb=identb,
                oh8=oh8, pq=pq, pq_bf=pq_bf, cc_done=cc_done, onesr=onesr,
                pool_load=pool_load, pool_done=pool_done, wq_pre=wq_pre,
                xn_t=xn_t_h, passa_pe=passa_pe, passa_dve=passa_dve,
            ))
        else:
            def f_fin(s):
                E.w(s, "sspill0", E.C.get("sspill0", 0))
                E.w(s, "sspill1", E.C.get("sspill1", 0))
                E.w(s, "sccd", E.C.get("sccd", 0))

            block.sync(f_fin)

    return nc


def _emit_pass_b(nc, block, E, L, R):
    pq = R["pq"]
    pq_bf = R["pq_bf"]
    pool = R["pool"]
    kv_bf = R["kv_bf"].ap()
    rows_b = R["rows_b"].ap()
    rows_f = R["rows_f"].ap()
    onesr = R["onesr"]
    cc_done = R["cc_done"]

    A = nc.alloc_sbuf_tensor
    y_acc = A("y_acc", [128, NCH * TILE], F32)
    xn_t = R["xn_t"]
    aoh = A("aoh", [128, NHEAD * TILE], BF16)
    qsb = [A(f"qsb{p}", [128, TILE], BF16) for p in range(2)]
    zbc = [A(f"zbc{p}", [128, TILE], BF16) for p in range(2)]
    silb = [A(f"silb{p}", [128, TILE], BF16) for p in range(2)]
    rarena = A("rarena", [128, 512], F32)
    comb_bf = A("comb_bf", [128, 64], BF16)
    combfm = A("combfm", [8, TILE], BF16)
    xev = [A(f"xev{p}", [128, TILE], F32) for p in range(2)]

    prev_store = {0: 0, 1: 0}
    prev_store_sem = {0: "sst0", 1: "sst1"}
    prev_yev = {0: 0, 1: 0}
    pool_load = R["pool_load"]
    pool_done = R["pool_done"]

    LAST = NBLK - 1

    for t in range(NTILE):
        T0 = t * TILE

        def f_ld0(s, t=t, T0=T0):
            E.w(s, "dve", L.get(("zeps", (NHEAD - 1, t - 1))))
            E.inc(s.dma_start(rows_f[0:1, 640:640 + TILE],
                              R["pos_row"][0:1, T0:T0 + TILE]), "spos", 16)

        block.sync(f_ld0)

        if t > 0:
            wq_slots_pre = {}
            for h in range(4):
                wq_slots_pre[h] = pool_load(
                    R["wq_r"][:, :, h * 128:(h + 1) * 128])

            def f_ld0b(s, t=t, T0=T0):
                E.w(s, "pe", L.get(("xnfree", t - 1)))
                E.inc(s.dma_start(
                    xn_t.ap()[:, 11 * TILE:].rearrange(
                        "p (c u) -> p c u", c=5),
                    R["xn_r"][:, 11:16, T0:T0 + TILE]), "sxnC", 16)

            block.sync(f_ld0b)

            def f_ld0c(g, t=t, T0=T0):
                E.w(g, "pe", L.get(("xnfree", t - 1)))
                E.inc(g.dma_start(
                    xn_t.ap()[:, 0:6 * TILE].rearrange(
                        "p (c u) -> p c u", c=6),
                    R["xn_r"][:, 0:6, T0:T0 + TILE]), "sxnB", 16)

            block.gpsimd(f_ld0c)

            def f_ld0d(sc, t=t, T0=T0):
                E.w(sc, "pe", L.get(("xnfree", t - 1)))
                E.inc(sc.dma_start(
                    xn_t.ap()[:, 6 * TILE:11 * TILE].rearrange(
                        "p (c u) -> p c u", c=5),
                    R["xn_r"][:, 6:11, T0:T0 + TILE]), "sxnD", 16)

            block.scalar(f_ld0d)
        xn_ready = E.C["sxnB"]
        xn_ready2 = E.C["sxnC"]
        xn_ready3 = E.C.get("sxnD", 0)
        pos_ready = E.C["spos"]

        def f_epsa(sc, t=t, pos_ready=pos_ready):
            erow = rows_f[0:1, 640:640 + TILE]
            E.w(sc, "spos", pos_ready)
            E.w(sc, "dve", L.get(("zeps", (NHEAD - 1, t - 1))))
            ins = sc.activation(erow, erow, AF.Exp, scale=1.0 / SEQ)
            L[("epsx", t)] = E.inc(ins, "act")

        block.scalar(f_epsa)

        def f_epsb(v, t=t):
            erow = rows_f[0:1, 640:640 + TILE]
            E.w(v, "act", L[("epsx", t)])
            ins = v.tensor_scalar_mul(erow, erow, Z_EPS)
            L[("epsr", t)] = E.inc(ins, "dve")
            E.w(v, "dve", E.C["dve"])
            E.w(v, "pe", L.get(("zmm", (NHEAD - 1, t - 1))))
            with nc.allow_low_precision(reason="z eps term, tiny relative"):
                ins = v.tensor_copy(
                    rows_b[0:1, 2176 + TILE:2176 + 2 * TILE], erow)
            L[("epsb", t)] = E.inc(ins, "dve")

        block.vector(f_epsb)

        if t == 0:
            wq_slots = dict(R["wq_pre"])
        else:
            wq_slots = wq_slots_pre

        for it in range(NHEAD + 1):
            h = it
            hp = it - 1
            if hp >= 0:
                def f_z(tt, hp=hp, t=t):
                    E.w(tt, "act", L[("qrelu", (hp, t))])
                    E.w(tt, "dve", L.get(("zeps", (hp - 1, t))))
                    if hp == 0:
                        if t == 0:
                            E.w(tt, "sccd", R["cc_done"])
                        E.w(tt, "act", L.get(("sig", t - 1)))
                        E.w(tt, "dve", L[("epsb", t)])
                    ins = None
                    for n in range(2):
                        tt.matmul(
                            pq[1].ap()[0:1, n * 512:(n + 1) * 512],
                            kv_bf[:, hp * 132 + 128:hp * 132 + 129],
                            qsb[hp % 2].ap()[:, n * 512:(n + 1) * 512],
                            start=True, stop=False)
                        ins = tt.matmul(
                            pq[1].ap()[0:1, n * 512:(n + 1) * 512],
                            onesr[0:1, 0:1],
                            rows_b[0:1, 2176 + TILE + n * 512:
                                   2176 + TILE + (n + 1) * 512],
                            start=False, stop=True)
                    L[("zmm", (hp, t))] = E.inc(ins, "pe")

                block.tensor(f_z)

            if h < NHEAD:
                def f_q(tt, h=h, t=t, xn_ready=xn_ready,
                        xn_ready2=xn_ready2, xn_ready3=xn_ready3):
                    slot, wv = wq_slots[h]
                    E.w(tt, f"wl{slot}", wv)
                    E.w(tt, "sxnB", xn_ready)
                    E.w(tt, "sxnC", xn_ready2)
                    if xn_ready3:
                        E.w(tt, "sxnD", xn_ready3)
                    E.w(tt, "act", L.get(("qrelu", (h - 1, t))))
                    if h == 0:
                        if t == 0:
                            E.w(tt, "dve", R["passa_dve"])
                        E.w(tt, "dve",
                            L.get(("yadd", (LAST, NCH - 1, t - 1))))
                    ins = None
                    for n in range(2):
                        for c in range(NCH):
                            ins = tt.matmul(
                                pq[0].ap()[:, n * 512:(n + 1) * 512],
                                pool[slot].ap()[:, c * 128:(c + 1) * 128],
                                xn_t.ap()[:, c * TILE + n * 512:
                                          c * TILE + (n + 1) * 512],
                                start=(c == 0), stop=(c == NCH - 1))
                    L[("qps", (h, t))] = E.inc(ins, "pe")
                    pool_done(slot)

                block.tensor(f_q)

                nxt = h + (POOL_N if t == 0 else 4)
                if nxt < NHEAD and nxt not in wq_slots:
                    wq_slots[nxt] = pool_load(
                        R["wq_r"][:, :, nxt * 128:(nxt + 1) * 128])

                def f_qrelu(sc, h=h, t=t):
                    E.w(sc, "pe", L[("qps", (h, t))])
                    E.w(sc, "pe", L.get(("smm", (h - 2, t))))
                    ins = sc.activation(qsb[h % 2].ap(), pq[0].ap(), AF.Relu,
                                        bias=R["biasq"].ap()[:, h:h + 1])
                    L[("qrelu", (h, t))] = E.inc(ins, "act")

                block.scalar(f_qrelu)

            if hp >= 0:
                def f_zchain(v, hp=hp, t=t):
                    pz = hp % 2
                    zr_b = rows_b[0:1, 128 + pz * TILE:
                                  128 + (pz + 1) * TILE]
                    E.w(v, "pe", L[("zmm", (hp, t))])
                    E.w(v, "pe", L.get(("zbmm", (hp - 2, t))))
                    with nc.allow_low_precision(reason="z row feeds bf16 mm"):
                        ins = v.reciprocal(zr_b, pq[1].ap()[0:1, 0:TILE])
                    L[("zeps", (hp, t))] = E.inc(ins, "dve")
                    L[("zrec", (hp, t))] = L[("zeps", (hp, t))]

                block.vector(f_zchain)

                def f_zb(tt, hp=hp, t=t):
                    E.w(tt, "dve", L.get(("ao", (hp - 1, t))))
                    if hp == 0:
                        E.w(tt, "dve",
                            L.get(("yadd", (LAST, NCH - 2, t - 1))))
                    ins = None
                    for n in range(2):
                        ins = tt.matmul(
                            pq[3].ap()[:, n * 512:(n + 1) * 512],
                            kv_bf[:, hp * 132:hp * 132 + 128],
                            qsb[hp % 2].ap()[:, n * 512:(n + 1) * 512],
                            start=True, stop=True)
                    L[("smm", (hp, t))] = E.inc(ins, "pe")
                    E.w(tt, "dve", L[("zrec", (hp, t))])
                    E.w(tt, "act", L.get(("zbcc", (hp - 1, t))))
                    if hp == 0:
                        E.w(tt, "dve", L.get(("hsc", (LAST, FCM - 1, t - 1))))
                    pz = hp % 2
                    zr_b = rows_b[0:1, 128 + pz * TILE:
                                  128 + (pz + 1) * TILE]
                    for n in range(2):
                        ins = tt.matmul(
                            pq[2].ap()[:, n * 512:(n + 1) * 512],
                            onesr,
                            zr_b[0:1, n * 512:(n + 1) * 512],
                            start=True, stop=True)
                    L[("zbmm", (hp, t))] = E.inc(ins, "pe")

                block.tensor(f_zb)

                def f_zbc(sc, hp=hp, t=t):
                    E.w(sc, "pe", L[("zbmm", (hp, t))])
                    E.w(sc, "dve", L.get(("ao", (hp - 2, t))))
                    ins = sc.copy(zbc[hp % 2].ap(), pq[2].ap())
                    L[("zbcc", (hp, t))] = E.inc(ins, "act")

                block.scalar(f_zbc)

                def f_ao(v, hp=hp, t=t):
                    E.w(v, "pe", L[("smm", (hp, t))])
                    E.w(v, "act", L[("zbcc", (hp, t))])
                    if hp == 0:
                        E.w(v, "pe", L.get(("womm", (NCH - 1, t - 1))))
                    ins = v.tensor_mul(
                        aoh.ap()[:, hp * TILE:(hp + 1) * TILE],
                        zbc[hp % 2].ap(), pq[3].ap())
                    L[("ao", (hp, t))] = E.inc(ins, "dve")

                block.vector(f_ao)

        wo_slots = {}
        for cc in range(2):
            wo_slots[cc] = pool_load(R["wo_r"][:, :, cc * 128:(cc + 1) * 128])
        for cc in range(NCH):
            if cc + 2 < NCH:
                wo_slots[cc + 2] = pool_load(
                    R["wo_r"][:, :, (cc + 2) * 128:(cc + 3) * 128])

            def f_wo(tt, cc=cc, t=t):
                slot, wv = wo_slots[cc]
                E.w(tt, f"wl{slot}", wv)
                bank = pq[cc % 2].ap()
                E.w(tt, "dve", L.get(("woev", (cc - 2, t))))
                if cc == 0:
                    E.w(tt, "act", L[("qrelu", (NHEAD - 1, t))])
                    E.w(tt, "dve", L[("ao", (NHEAD - 1, t))])
                if cc == 1:
                    E.w(tt, "dve", L[("zeps", (NHEAD - 1, t))])
                ins = None
                for n in range(2):
                    for hc in range(NHEAD):
                        ins = tt.matmul(
                            bank[:, n * 512:(n + 1) * 512],
                            pool[slot].ap()[:, hc * 128:(hc + 1) * 128],
                            aoh.ap()[:, hc * TILE + n * 512:
                                     hc * TILE + (n + 1) * 512],
                            start=(hc == 0), stop=(hc == NHEAD - 1))
                L[("womm", (cc, t))] = E.inc(ins, "pe")
                pool_done(slot)

            block.tensor(f_wo)

            def f_woev(v, cc=cc, t=t):
                E.w(v, "pe", L[("womm", (cc, t))])
                E.w(v, f"sst{cc % 2}", L.get(("yfree", (cc, t - 1)), 0))
                ins = v.tensor_scalar_add(
                    y_acc.ap()[:, cc * TILE:(cc + 1) * TILE],
                    pq[cc % 2].ap(), R["biaso"].ap()[:, cc:cc + 1])
                L[("woev", (cc, t))] = E.inc(ins, "dve")

            block.vector(f_woev)

        def f_router_mm(tt, t=t):
            E.w(tt, "act", L[("zbcc", (NHEAD - 1, t))])
            E.w(tt, "dve", L.get(("comb", t - 1)))
            ins = None
            for t8 in range(8):
                for c in range(NCH):
                    ins = tt.matmul(
                        pq[2].ap()[:, t8 * 8:(t8 + 1) * 8],
                        xn_t.ap()[:, c * TILE + t8 * 128:
                                  c * TILE + (t8 + 1) * 128],
                        R["gate_b"].ap()[:, c * 8:(c + 1) * 8],
                        start=(c == 0), stop=(c == NCH - 1))
            L[("rmm", t)] = E.inc(ins, "pe")
            E.w(tt, "dve", L[("zeps", (NHEAD - 1, t))])
            E.w(tt, "dve", L[("woev", (NCH - 1, t))])
            for n in range(2):
                for c in range(NCH):
                    ins = tt.matmul(
                        pq[1].ap()[0:1, n * 512:(n + 1) * 512],
                        R["sg_b"].ap()[:, c:c + 1],
                        xn_t.ap()[:, c * TILE + n * 512:
                                  c * TILE + (n + 1) * 512],
                        start=(c == 0), stop=(c == NCH - 1))
            L[("gmm", t)] = E.inc(ins, "pe")

        block.tensor(f_router_mm)

        def f_sig(sc, t=t):
            E.w(sc, "pe", L[("gmm", t)])
            E.w(sc, "pe", L.get(("gbmm", t - 1)))
            ins = sc.activation(rows_b[0:1, 2176:2176 + TILE],
                                pq[1].ap()[0:1, 0:TILE], AF.Sigmoid)
            L[("sig", t)] = E.inc(ins, "act")

        block.scalar(f_sig)

        RA = rarena.ap()
        probs3 = RA[:, 0:64].rearrange("p (a b) -> p a b", a=8)
        st1 = RA[:, 64:72].rearrange("p (a b) -> p a b", b=1)
        st2 = RA[:, 72:80].rearrange("p (a b) -> p a b", b=1)
        wrk3 = RA[:, 128:192].rearrange("p (a b) -> p a b", a=8)
        comb3 = RA[:, 192:256].rearrange("p (a b) -> p a b", a=8)

        def f_rsoft1(v, t=t):
            rps = pq[2].ap()[:, 0:64].rearrange("p (a b) -> p a b", a=8)

            E.w(v, "pe", L[("rmm", t)])
            E.w(v, "dve", L.get(("comb_bf", t - 1)))
            ins = v.tensor_reduce(st1, rps, axis=AX.X, op=ALU.max,
                                  negate=True)
            E.inc(ins, "dve")
            E.w(v, "dve", E.C["dve"])
            ins = v.tensor_tensor(probs3, rps,
                                  st1.broadcast_to((128, 8, 8)), op=ALU.add)
            L[("rsub", t)] = E.inc(ins, "dve")

        block.vector(f_rsoft1)

        def f_rexp(sc, t=t):
            E.w(sc, "dve", L[("rsub", t)])
            ins = sc.activation(RA[:, 0:64], RA[:, 0:64], AF.Exp)
            L[("rexp", t)] = E.inc(ins, "act")

        block.scalar(f_rexp)

        def f_rsoft2(v, t=t):
            def step(ins):
                E.inc(ins, "dve")
                E.w(v, "dve", E.C["dve"])

            E.w(v, "act", L[("rexp", t)])
            step(v.tensor_reduce(st1, probs3, axis=AX.X, op=ALU.add))
            step(v.reciprocal(st1, st1))
            step(v.tensor_tensor(probs3, probs3,
                                 st1.broadcast_to((128, 8, 8)), op=ALU.mult))
            step(v.tensor_reduce(st2, probs3, axis=AX.X, op=ALU.max))
            step(v.tensor_tensor(wrk3, probs3,
                                 st2.broadcast_to((128, 8, 8)),
                                 op=ALU.is_equal))
            step(v.scalar_tensor_tensor(wrk3, wrk3, -1e9, probs3,
                                        op0=ALU.mult, op1=ALU.add))
            step(v.tensor_reduce(st2, wrk3, axis=AX.X, op=ALU.max))
            step(v.tensor_tensor(wrk3, probs3,
                                 st2.broadcast_to((128, 8, 8)),
                                 op=ALU.is_ge))
            ins = v.tensor_tensor(comb3, probs3, wrk3, op=ALU.mult)
            L[("comb", t)] = E.inc(ins, "dve")
            E.w(v, "dve", E.C["dve"])
            E.w(v, "pe", L.get(("ctr", t - 1)))
            ins = v.tensor_copy(comb_bf.ap(), RA[:, 192:256])
            L[("comb_bf", t)] = E.inc(ins, "dve")

        block.vector(f_rsoft2)

        def f_ctr(tt, t=t):
            E.w(tt, "dve", L[("woev", (NCH - 2, t))])
            E.w(tt, "dve", L[("comb_bf", t)])
            ins = None
            for t8 in range(8):
                ins = tt.transpose(
                    pq_bf[0][0:8, t8 * 128:(t8 + 1) * 128],
                    comb_bf.ap()[:, t8 * 8:(t8 + 1) * 8],
                    R["identb"].ap())
            L[("ctr", t)] = E.inc(ins, "pe")

        block.tensor(f_ctr)

        def f_cfm(v, t=t):
            E.w(v, "pe", L[("ctr", t)])
            E.w(v, "pe", L.get(("cbmm", (NEXP - 1, t - 1))))
            ins = v.tensor_copy(combfm.ap(), pq_bf[0][0:8, 0:TILE])
            L[("cfm", t)] = E.inc(ins, "dve")

        block.vector(f_cfm)

        def f_gb(tt, t=t):
            E.w(tt, "act", L[("sig", t)])
            E.w(tt, "dve", L[("rsub", t)])
            E.w(tt, "dve", L.get(("hsc", (LAST, FCM - 1, t - 1))))
            ins = None
            for n in range(2):
                ins = tt.matmul(
                    pq[2].ap()[:, n * 512:(n + 1) * 512],
                    onesr, rows_b[0:1, 2176 + n * 512:2176 + (n + 1) * 512],
                    start=True, stop=True)
            L[("gbmm", t)] = E.inc(ins, "pe")

        block.tensor(f_gb)

        def emit_block(bi, t):
            shared = bi < NSHBLK
            e = bi - NSHBLK

            if shared:
                def w1(fc):
                    return R["sfc1_r"][:, :, bi * FMOE + fc * 128:
                                       bi * FMOE + (fc + 1) * 128]

                def w2(fc):
                    return R["sfc2_r"][:, :, bi * FMOE + fc * 128:
                                       bi * FMOE + (fc + 1) * 128]

                def wp(cc):
                    return R["sproj_r"][:, bi * FCM:(bi + 1) * FCM,
                                        cc * 128:(cc + 1) * 128]
            else:
                def w1(fc):
                    return R["fc1_r"][e, :, :, fc * 128:(fc + 1) * 128]

                def w2(fc):
                    return R["fc2_r"][e, :, :, fc * 128:(fc + 1) * 128]

                def wp(cc):
                    return R["proj_r"][e, :, :, cc * 128:(cc + 1) * 128]

            if not shared:
                def f_cb(tt, e=e, bi=bi, t=t):
                    E.w(tt, "dve", L[("cfm", t)])
                    E.w(tt, "dve", L[("hsc", (bi - 1, FCM - 1, t))])
                    ins = None
                    for n in range(2):
                        ins = tt.matmul(
                            pq[2].ap()[:, n * 512:(n + 1) * 512],
                            R["oh8"].ap()[0:8, e * 128:(e + 1) * 128],
                            combfm.ap()[0:8, n * 512:(n + 1) * 512],
                            start=True, stop=True)
                    L[("cbmm", (e, t))] = E.inc(ins, "pe")

                block.tensor(f_cb)
                scale_lm = ("pe", L[("cbmm", (e, t))])
            else:
                scale_lm = ("pe", L[("gbmm", t)])

            slots1 = {}
            slots2 = {}
            for fc in range(2):
                slots1[fc] = pool_load(w1(fc))
                slots2[fc] = pool_load(w2(fc))

            for fc in range(FCM):
                if fc + 2 < FCM:
                    slots1[fc + 2] = pool_load(w1(fc + 2))
                    slots2[fc + 2] = pool_load(w2(fc + 2))

                def f_h12(tt, bi=bi, fc=fc, t=t):
                    s1, wv1 = slots1[fc]
                    s2, wv2 = slots2[fc]
                    E.w(tt, f"wl{s1}", wv1)
                    E.w(tt, "act", L.get(("sil", (bi, fc - 1, t))))
                    if fc == 0:
                        if bi == 0:
                            E.w(tt, "dve", L[("cfm", t)])
                        else:
                            E.w(tt, "dve", L[("yadd", (bi - 1, NCH - 1, t))])
                    ins = None
                    for n in range(2):
                        for c in range(NCH):
                            ins = tt.matmul(
                                pq[0].ap()[:, n * 512:(n + 1) * 512],
                                pool[s1].ap()[:, c * 128:(c + 1) * 128],
                                xn_t.ap()[:, c * TILE + n * 512:
                                          c * TILE + (n + 1) * 512],
                                start=(c == 0), stop=(c == NCH - 1))
                    L[("h1", (bi, fc, t))] = E.inc(ins, "pe")
                    pool_done(s1)
                    E.w(tt, f"wl{s2}", wv2)
                    E.w(tt, "dve", L.get(("hmul", (bi, fc - 1, t))))
                    if fc == 0 and bi == 0:
                        E.w(tt, "act", L[("sig", t)])
                    for n in range(2):
                        for c in range(NCH):
                            ins = tt.matmul(
                                pq[1].ap()[:, n * 512:(n + 1) * 512],
                                pool[s2].ap()[:, c * 128:(c + 1) * 128],
                                xn_t.ap()[:, c * TILE + n * 512:
                                          c * TILE + (n + 1) * 512],
                                start=(c == 0), stop=(c == NCH - 1))
                    L[("h2", (bi, fc, t))] = E.inc(ins, "pe")
                    pool_done(s2)
                    if bi == NBLK - 1 and fc == FCM - 1:
                        L[("xnfree", t)] = L[("h2", (bi, fc, t))]

                block.tensor(f_h12)

                def f_sil(sc, bi=bi, fc=fc, t=t):
                    E.w(sc, "pe", L[("h1", (bi, fc, t))])
                    E.w(sc, "dve", L.get(("hmul", (bi, fc - 2, t))))
                    ins = sc.activation(silb[fc % 2].ap(), pq[0].ap(),
                                        AF.Silu)
                    L[("sil", (bi, fc, t))] = E.inc(ins, "act")

                block.scalar(f_sil)

                def f_hmul(v, bi=bi, fc=fc, t=t, scale_lm=scale_lm):
                    hv = aoh.ap()[:, fc * TILE:(fc + 1) * TILE]
                    E.w(v, "act", L[("sil", (bi, fc, t))])
                    E.w(v, "pe", L[("h2", (bi, fc, t))])
                    if fc == 0:
                        if bi == 0:
                            E.w(v, "pe", L[("womm", (NCH - 1, t))])
                        else:
                            E.w(v, "pe", L[("projlast", (bi - 1, t))])
                    ins = v.tensor_mul(hv, silb[fc % 2].ap(), pq[1].ap())
                    L[("hmul", (bi, fc, t))] = E.inc(ins, "dve")
                    E.w(v, "dve", E.C["dve"])
                    E.w(v, scale_lm[0], scale_lm[1])
                    ins = v.tensor_mul(hv, hv, pq[2].ap())
                    L[("hsc", (bi, fc, t))] = E.inc(ins, "dve")

                block.vector(f_hmul)

            pslots = {}
            for cc in range(2):
                pslots[cc] = pool_load(wp(cc))
            for cc in range(NCH):
                if cc + 2 < NCH:
                    pslots[cc + 2] = pool_load(wp(cc + 2))

                def f_proj(tt, bi=bi, cc=cc, t=t):
                    s, wv = pslots[cc]
                    E.w(tt, f"wl{s}", wv)
                    bank = pq[3 if cc % 2 == 0 else 0].ap()
                    E.w(tt, "dve", L.get(("yadd", (bi, cc - 2, t))))
                    if cc == 0:
                        if bi == 0:
                            E.w(tt, "dve", L[("ao", (NHEAD - 1, t))])
                        else:
                            E.w(tt, "dve", L[("yadd", (bi - 1, NCH - 1, t))])
                    if cc == 1:
                        E.w(tt, "act", L[("sil", (bi, FCM - 1, t))])
                    ins = None
                    for fc in range(FCM):
                        if cc == 0:
                            E.w(tt, "dve", L[("hsc", (bi, fc, t))])
                        for n in range(2):
                            ins = tt.matmul(
                                bank[:, n * 512:(n + 1) * 512],
                                pool[s].ap()[:, fc * 128:(fc + 1) * 128],
                                aoh.ap()[:, fc * TILE + n * 512:
                                         fc * TILE + (n + 1) * 512],
                                start=(fc == 0), stop=(fc == FCM - 1))
                    L[("projmm", (cc, t))] = E.inc(ins, "pe")
                    if cc == NCH - 1:
                        L[("projlast", (bi, t))] = E.C["pe"]
                    pool_done(s)

                block.tensor(f_proj)

                def f_yadd(v, bi=bi, cc=cc, t=t):
                    bank = pq[3 if cc % 2 == 0 else 0].ap()
                    E.w(v, "pe", L[("projmm", (cc, t))])
                    E.w(v, "dve", L[("woev", (cc, t))])
                    ya = y_acc.ap()[:, cc * TILE:(cc + 1) * TILE]
                    ins = v.tensor_add(ya, ya, bank)
                    L[("yadd", (bi, cc, t))] = E.inc(ins, "dve")

                block.vector(f_yadd)

        for bi in range(NBLK):
            emit_block(bi, t)

        for cc in range(NCH):
            p = cc % 2

            def f_xld(s, cc=cc, t=t, p=p, T0=T0):
                if t == 0 and cc < 2:
                    E.w(s, "pe", R["passa_pe"])
                E.w(s, "dve", prev_yev[p])
                E.w(s, prev_store_sem[p], prev_store[p])
                E.inc(s.dma_start(
                    xev[p].ap().rearrange("p (a u) -> p a u", a=1),
                    R["x_r"][:, cc:cc + 1, T0:T0 + TILE]), f"sxe{p}", 16)

            block.sync(f_xld)
            xe_ready = E.C[f"sxe{p}"]

            def f_yev(v, cc=cc, t=t, p=p, xe_ready=xe_ready):
                E.w(v, f"sxe{p}", xe_ready)
                E.w(v, prev_store_sem[p], prev_store[p])
                E.w(v, "dve", L[("yadd", (LAST, cc, t))])
                ya = y_acc.ap()[:, cc * TILE:(cc + 1) * TILE]
                ins = v.tensor_add(xev[p].ap(), xev[p].ap(), ya)
                L[("yev", (cc, t))] = E.inc(ins, "dve")

            block.vector(f_yev)
            prev_yev[p] = L[("yev", (cc, t))]

            gp_store = (t == NTILE - 1 and p == 1)
            sem_st = "sstg" if gp_store else f"sst{p}"

            def f_st(s, cc=cc, t=t, p=p, T0=T0, sem_st=sem_st):
                E.w(s, "dve", L[("yev", (cc, t))])
                E.inc(s.dma_start(
                    R["y_r"][:, cc:cc + 1, T0:T0 + TILE],
                    xev[p].ap().rearrange("p (a u) -> p a u", a=1)),
                    sem_st, 16)

            if gp_store:
                block.gpsimd(f_st)
            else:
                block.sync(f_st)
            prev_store[p] = E.C[sem_st]
            prev_store_sem[p] = sem_st
            L[("yfree", (cc, t))] = E.C[sem_st]

    def f_fin(s):
        E.w(s, "sst0", E.C.get("sst0", 0))
        E.w(s, "sst1", E.C.get("sst1", 0))
        E.w(s, "sstg", E.C.get("sstg", 0))

    block.sync(f_fin)


_PROG_CACHE = {}


def _get_program():
    key = (DO_PASSB, DBG_TAPS, DEV_HALVES, DEV_CHUNKS, DEV_CC, DEV_KV_COPY)
    if key not in _PROG_CACHE:
        _PROG_CACHE[key] = build_core_program()
    return _PROG_CACHE[key]


LAST_RESULTS = None


def host_inputs(L0_attn_x, L0_attn_Wq, L0_attn_bq, L0_attn_Wk, L0_attn_bk,
                L0_attn_Wv, L0_attn_bv, L0_attn_Wo, L0_attn_bo,
                L0_ffn_gate_w, L0_ffn_expert_fc1, L0_ffn_expert_fc2,
                L0_ffn_expert_proj, L0_ffn_shared_fc1, L0_ffn_shared_fc2,
                L0_ffn_shared_proj, L0_ffn_shared_gate_w):
    f32 = np.float32

    def cT(a):
        return np.ascontiguousarray(np.asarray(a, dtype=f32).T)

    wqT = cT(L0_attn_Wq)
    wkT = cT(L0_attn_Wk)
    wvT = cT(L0_attn_Wv)
    woT = cT(L0_attn_Wo)
    bq_cols = np.ascontiguousarray(
        np.asarray(L0_attn_bq, f32).reshape(NCH, 128).T)
    bo_cols = np.ascontiguousarray(
        np.asarray(L0_attn_bo, f32).reshape(NCH, 128).T)
    bkv_rows = np.ascontiguousarray(
        np.concatenate([np.asarray(L0_attn_bk, f32),
                        np.asarray(L0_attn_bv, f32)])[None, :])
    gw = np.asarray(L0_ffn_gate_w, f32)
    gate_cols = np.ascontiguousarray(
        gw.T.reshape(NCH, 128, NEXP).transpose(1, 0, 2).reshape(128, NCH * NEXP))
    sg_cols = np.ascontiguousarray(
        np.asarray(L0_ffn_shared_gate_w, f32).reshape(NCH, 128).T)
    fc1T = np.ascontiguousarray(
        np.asarray(L0_ffn_expert_fc1, f32).transpose(0, 2, 1))
    fc2T = np.ascontiguousarray(
        np.asarray(L0_ffn_expert_fc2, f32).transpose(0, 2, 1))
    projT = np.ascontiguousarray(
        np.asarray(L0_ffn_expert_proj, f32).transpose(0, 2, 1))
    sfc1T = cT(L0_ffn_shared_fc1)
    sfc2T = cT(L0_ffn_shared_fc2)
    sprojT = cT(L0_ffn_shared_proj)
    ones_col = np.ones((128, 1), f32)
    ones_row_d = np.ones((1, 128), f32)
    ident_d = np.eye(128, dtype=f32)
    oh8_d = np.zeros((8, NEXP * 128), f32)
    for e in range(NEXP):
        oh8_d[e, e * 128:(e + 1) * 128] = 1.0

    x = np.asarray(L0_attn_x, f32)
    in_maps = []
    for core in range(8):
        b = core // 4
        p0 = (core % 4) * TCORE
        x_fm_a = np.ascontiguousarray(x[b, p0:p0 + TCORE, :].T)
        pos = (p0 + np.arange(TCORE, dtype=np.int64)).astype(f32)
        pos_row_a = np.ascontiguousarray(pos[None, :])
        pos_cols_a = np.ascontiguousarray(pos.reshape(NCHUNK, 128).T)
        in_maps.append(dict(
            x_fm=x_fm_a, pos_row=pos_row_a, pos_cols=pos_cols_a,
            wqT=wqT, wkT=wkT, wvT=wvT, woT=woT,
            bq_cols=bq_cols, bo_cols=bo_cols, bkv_rows=bkv_rows,
            gate_cols=gate_cols, sg_cols=sg_cols,
            fc1T=fc1T, fc2T=fc2T, projT=projT,
            sfc1T=sfc1T, sfc2T=sfc2T, sprojT=sprojT,
            ones_col=ones_col, ones_row_d=ones_row_d, ident_d=ident_d,
            oh8_d=oh8_d,
        ))
    return in_maps


def kernel(**inputs):
    global LAST_RESULTS
    in_maps = host_inputs(**inputs)
    nc = _get_program()
    res = run_bass_kernel_spmd(nc, in_maps[:DEV_CORES],
                               core_ids=list(range(DEV_CORES)))
    LAST_RESULTS = res
    if DEV_CORES < 8:
        return np.zeros((BATCH, SEQ, D), np.float32)

    out = np.empty((BATCH, SEQ, D), dtype=np.float32)
    for core in range(8):
        b = core // 4
        p0 = (core % 4) * TCORE
        out[b, p0:p0 + TCORE, :] = res.results[core]["y_fm"].T
    return out
